# version 21
# baseline (speedup 1.0000x reference)
import contextlib
import numpy as np
import ml_dtypes
import concourse.bass as bass
import concourse.mybir as mybir
from concourse.bass_utils import run_bass_kernel_spmd

F32 = mybir.dt.float32
BF16 = mybir.dt.bfloat16
AF = mybir.ActivationFunctionType
ALU = mybir.AluOpType
AX = mybir.AxisListType.X
bf16 = ml_dtypes.bfloat16

D = 2048
KC = 16
DIN = 4096
DST = 128
NH = 64
HD = 64
CHK = 256
CONVD = DIN + 2 * DST
NCC = CONVD // 128
PROJ = 8512
TOK = 1024
TOKT = 2048
NCH = 4
EPS = 1e-5
NE = 16
IE = 768


class Sem:
    def __init__(self, ctx, nc, name):
        self.h = ctx.enter_context(nc.semaphore(name))
        self.n = 0

    def inc(self, instr, k=1):
        instr.then_inc(self.h, k)
        self.n += k
        return self.n


def _bc(ap, axis, n):
    shp = list(ap.shape)
    shp.insert(axis, n)
    return ap.unsqueeze(axis).to_broadcast(shp)


def build(debug=False):
    nc = bass.Bass("TRN2", target_bir_lowering=False)

    din = {}
    for name, shp in [
        ("xin", [D, TOKT]), ("smask", [128, 1]), ("w1t", [D, PROJ]),
        ("convw", [128, NCC * 4]), ("convb", [128, NCC]),
        ("convd", [5 * NCC * 128, 128]),
        ("dtb", [NH, 1]), ("alog", [NH, 1]), ("dpp", [128, 32]),
        ("normw", [128, 32]), ("w2t", [DIN, D]), ("wrt", [D, NE]),
        ("ebias", [1, NE]), ("gut", [NE, D, 2 * IE]), ("dwt", [NE, IE, D]),
        ("sgut", [D, 2 * IE]), ("sdt", [IE, D]),
    ]:
        din[name] = nc.dram_tensor(name, shp, F32, kind="ExternalInput").ap()

    out_hbm = nc.dram_tensor("out", [D, TOK], F32, kind="ExternalOutput").ap()

    skind = "ExternalOutput" if debug else "Internal"
    xbc_scr = nc.dram_tensor("xbc_scr", [CONVD, 4 + TOKT], BF16, kind=skind).ap()
    z_scr = nc.dram_tensor("z_scr", [DIN, TOK], BF16, kind=skind).ap()
    x2_scr = nc.dram_tensor("x2_scr", [D, TOK], F32, kind=skind).ap()
    yn_scr = nc.dram_tensor("yn_scr", [DIN, TOK], BF16, kind=skind).ap()
    ead_scr = nc.dram_tensor("ead_scr", [NH, TOKT], F32, kind=skind).ap()
    eac_scr = nc.dram_tensor("eac_scr", [NH, TOKT], F32, kind=skind).ap()
    cdr_scr = nc.dram_tensor("cdr_scr", [2 * NCH, NH], F32, kind=skind).ap()
    dt_scr = nc.dram_tensor("dt_scr", [NH, TOKT], F32, kind=skind).ap() if debug else None
    mr_scr = nc.dram_tensor("mr_scr", [2, TOKT], F32, kind="Internal").ap()
    dtT_scr = nc.dram_tensor("dtT_scr", [128, 2 * NCH * 2 * NH], F32, kind="Internal").ap()
    dsT_scr = nc.dram_tensor("dsT_scr", [128, 2 * NCH * 2 * NH], F32, kind="Internal").ap()
    r2_scr = nc.dram_tensor("r2_scr", [2, TOK], F32, kind="Internal").ap()
    rr_scr = nc.dram_tensor("rr_scr", [1, CHK], F32, kind="Internal").ap()
    cmb_scr = nc.dram_tensor("cmb_scr", [NE, TOK], F32, kind="Internal").ap()
    dbg = {}
    if debug:
        for name, shp in [("d_xln", [D, TOKT]), ("d_state", [128, NH * HD]),
                          ("d_y", [DIN, TOK]),
                          ("d_xln2", [D, TOK]), ("d_cmb", [NE, TOK]),
                          ("d_scores", [NE, TOK]), ("d_xconv", [CONVD, CHK])]:
            dbg[name] = nc.dram_tensor(name, shp, F32, kind="ExternalOutput").ap()
        dbg["dt_scr"] = dt_scr

    ones_col = nc.inline_tensor(np.ones((128, 1), dtype=bf16), "ones_col").ap()
    ones_col_f = nc.inline_tensor(np.ones((128, 1), dtype=np.float32), "ones_col_f").ap()
    ones_w = nc.inline_tensor(np.ones((128, 256), dtype=bf16), "ones_w").ap()
    id_bf = nc.inline_tensor(np.eye(128, dtype=bf16), "id_bf").ap()
    id_f32 = nc.inline_tensor(np.eye(128, dtype=np.float32), "id_f32").ap()
    dg0_c = nc.inline_tensor(np.eye(128, 256, dtype=bf16), "dg0").ap()
    dg1_c = nc.inline_tensor(np.eye(128, 128, dtype=bf16), "dg1").ap()

    ctx = contextlib.ExitStack()
    E = ctx.enter_context

    sT = Sem(ctx, nc, "sT")
    sV = Sem(ctx, nc, "sV")
    sS = Sem(ctx, nc, "sS")
    sP = Sem(ctx, nc, "sP")
    dX = [Sem(ctx, nc, "dX0"), Sem(ctx, nc, "dX1")]
    dW = [Sem(ctx, nc, "dW0"), Sem(ctx, nc, "dW1")]
    dS = [Sem(ctx, nc, "dS0"), Sem(ctx, nc, "dS1")]
    dR = Sem(ctx, nc, "dR")
    dB = Sem(ctx, nc, "dB")
    dE = [Sem(ctx, nc, "dE0"), Sem(ctx, nc, "dE1")]
    dEs = [Sem(ctx, nc, "dEs0"), Sem(ctx, nc, "dEs1")]
    dC = Sem(ctx, nc, "dC")
    dO = Sem(ctx, nc, "dO")
    dY = Sem(ctx, nc, "dY")
    dG = Sem(ctx, nc, "dG")
    dRT = Sem(ctx, nc, "dRT")
    dW2 = [Sem(ctx, nc, "dW2_0"), Sem(ctx, nc, "dW2_1")]
    dC2 = Sem(ctx, nc, "dC2")
    dZ = Sem(ctx, nc, "dZ")
    dGA = [Sem(ctx, nc, "dGA0"), Sem(ctx, nc, "dGA1")]
    dGB = [Sem(ctx, nc, "dGB0"), Sem(ctx, nc, "dGB1")]

    g_ones = E(nc.sbuf_tensor("g_ones", [128, 1], BF16))
    g_onesf = E(nc.sbuf_tensor("g_onesf", [128, 1], F32))
    g_onesw = E(nc.sbuf_tensor("g_onesw", [128, 256], BF16))
    g_idbf = E(nc.sbuf_tensor("g_idbf", [128, 128], BF16))
    g_idf32 = E(nc.sbuf_tensor("g_idf32", [128, 128], F32))
    g_dg0 = E(nc.sbuf_tensor("g_dg0", [128, 256], BF16))
    g_dg1 = E(nc.sbuf_tensor("g_dg1", [128, 128], BF16))
    g_smask = E(nc.sbuf_tensor("g_smask", [128, 1], F32))
    g_convw = E(nc.sbuf_tensor("g_convw", [128, NCC, 4], F32))
    g_convb = E(nc.sbuf_tensor("g_convb", [128, NCC], F32))
    g_dtb = E(nc.sbuf_tensor("g_dtb", [NH, 1], F32))
    g_alog = E(nc.sbuf_tensor("g_alog", [NH, 1], F32))
    g_dpp = E(nc.sbuf_tensor("g_dpp", [128, 32], F32))
    g_normw = E(nc.sbuf_tensor("g_normw", [128, 32], F32))

    c_dtT = contextlib.ExitStack()
    g_dtT = c_dtT.enter_context(nc.sbuf_tensor("g_dtT", [128, 2 * NCH, 2, NH], F32))
    g_dsT = c_dtT.enter_context(nc.sbuf_tensor("g_dsT", [128, 2 * NCH, 2, NH], F32))
    diagw = c_dtT.enter_context(nc.sbuf_tensor("diagw", [128, 5, NCC, 128], BF16,
                                               side="right"))
    c_dtr = contextlib.ExitStack()
    g_dt_raw = c_dtr.enter_context(nc.sbuf_tensor("g_dt_raw", [NH, TOKT], F32))
    c_xln = contextlib.ExitStack()
    xln = c_xln.enter_context(nc.sbuf_tensor("xln", [128, KC, TOKT], BF16))

    with contextlib.ExitStack() as c0:
        e0 = c0.enter_context
        xf = [e0(nc.sbuf_tensor(f"xf{i}", [128, TOKT], F32)) for i in range(2)]
        sqb = e0(nc.sbuf_tensor("sqb", [128, TOKT], BF16))
        srow = e0(nc.sbuf_tensor("srow", [1, 3 * 512], F32))
        mB = e0(nc.sbuf_tensor("mB", [128, TOKT], BF16))
        rB = e0(nc.sbuf_tensor("rB", [128, TOKT], BF16))
        t1b = [e0(nc.sbuf_tensor(f"t1b{i}", [128, TOKT], BF16)) for i in range(2)]
        wblk = [e0(nc.sbuf_tensor(f"wblk{i}", [128, KC, 512], BF16)) for i in range(2)]
        stg = [e0(nc.sbuf_tensor(f"stg{i}", [128, TOKT + 4], BF16)) for i in range(2)]
        cst = contextlib.ExitStack()
        pstat = [cst.enter_context(nc.psum_tensor(f"pstat{i}", [1, 512], F32)) for i in range(8)]

        with nc.Block() as B:
            def ld_consts(sync):
                for dst, src in [(g_ones, ones_col), (g_onesf, ones_col_f),
                                 (g_onesw, ones_w), (g_idbf, id_bf),
                                 (g_idf32, id_f32), (g_dg0, dg0_c), (g_dg1, dg1_c),
                                 (g_smask, din["smask"]), (g_dtb, din["dtb"]),
                                 (g_alog, din["alog"]), (g_dpp, din["dpp"]),
                                 (g_normw, din["normw"]), (g_convb, din["convb"])]:
                    dX[0].inc(sync.dma_start(out=dst[:], in_=src[:]), 16)
                dX[0].inc(sync.dma_start(
                    out=g_convw[:], in_=din["convw"][:].rearrange("p (c t) -> p c t", t=4)), 16)
            B.sync(ld_consts)


            sc_marks = []
            cp_marks = []
            pe_marks = []
            for fc in range(KC):
                sl = fc % 2

                def ld_x(sync, fc=fc, sl=sl):
                    if fc >= 2:
                        sync.wait_ge(sS.h, sc_marks[fc - 2])
                        sync.wait_ge(sV.h, cp_marks[fc - 2])
                    dX[sl].inc(sync.dma_start(
                        out=xf[sl][:], in_=din["xin"][fc * 128:(fc + 1) * 128, :]), 16)
                B.sync(ld_x)

                def do_cp(vector, fc=fc, sl=sl):
                    vector.wait_ge(dX[sl].h, dX[sl].n)
                    cp_marks.append(sV.inc(vector.tensor_copy(xln[:, fc, :], xf[sl][:])))
                B.vector(do_cp)

                def do_sq(scalar, fc=fc, sl=sl):
                    scalar.wait_ge(dX[sl].h, dX[sl].n)
                    if fc >= 1:
                        scalar.wait_ge(sT.h, pe_marks[fc - 1])
                    sc_marks.append(sS.inc(scalar.activation(sqb[:], xf[sl][:], AF.Square)))
                B.scalar(do_sq)

                def do_pe(tensor, fc=fc):
                    tensor.wait_ge(sS.h, sc_marks[fc])
                    tensor.wait_ge(sV.h, cp_marks[fc])
                    for ts in range(4):
                        tensor.matmul(pstat[ts][:, :], g_ones[:, :],
                                      xln[:, fc, ts * 512:(ts + 1) * 512],
                                      start=(fc == 0), stop=(fc == KC - 1))
                    for ts in range(4):
                        mm = tensor.matmul(pstat[4 + ts][:, :], g_ones[:, :],
                                           sqb[:, ts * 512:(ts + 1) * 512],
                                           start=(fc == 0), stop=(fc == KC - 1))
                    pe_marks.append(sT.inc(mm))
                B.tensor(do_pe)

            for ts in range(4):
                def fin_ts(vector, ts=ts):
                    if ts == 0:
                        vector.wait_ge(sT.h, pe_marks[-1])
                    if ts >= 1:
                        vector.wait_ge(dR.h, 32 * ts)
                    vector.tensor_scalar(srow[0:1, 0:512], pstat[ts][:, :], 1.0 / D, None, op0=ALU.mult)
                    m1 = sV.inc(vector.tensor_scalar(srow[0:1, 512:1024], pstat[4 + ts][:, :],
                                                     1.0 / D, None, op0=ALU.mult))
                    vector.wait_ge(sV.h, m1)
                    m2 = sV.inc(vector.tensor_mul(srow[0:1, 1024:1536], srow[0:1, 0:512], srow[0:1, 0:512]))
                    vector.wait_ge(sV.h, m2)
                    sV.inc(vector.tensor_sub(srow[0:1, 512:1024], srow[0:1, 512:1024], srow[0:1, 1024:1536]))
                B.vector(fin_ts)
                VD = sV.n

                def sq_ts(scalar, ts=ts, VD=VD):
                    scalar.wait_ge(sV.h, VD)
                    sS.inc(scalar.activation(srow[0:1, 1024:1536], srow[0:1, 512:1024],
                                             AF.Sqrt, bias=np.float32(EPS)))
                B.scalar(sq_ts)
                SD = sS.n

                def rc_ts(vector, ts=ts, SD=SD):
                    vector.wait_ge(sS.h, SD)
                    sV.inc(vector.reciprocal(srow[0:1, 512:1024], srow[0:1, 1024:1536]))
                B.vector(rc_ts)
                RD = sV.n

                def st_ts(sync, ts=ts, RD=RD):
                    sync.wait_ge(sV.h, RD)
                    dR.inc(sync.dma_start(out=mr_scr[0:1, ts * 512:(ts + 1) * 512],
                                          in_=srow[0:1, 0:512]), 16)
                    dR.inc(sync.dma_start(out=mr_scr[1:2, ts * 512:(ts + 1) * 512],
                                          in_=srow[0:1, 512:1024]), 16)
                B.sync(st_ts)

            def row_wait(sync):
                sync.wait_ge(dR.h, dR.n)
            B.sync(row_wait)

            def row_bcast(gpsimd):
                gpsimd.wait_ge(dR.h, dR.n)
                dG.inc(gpsimd.dma_start(out=mB[:], in_=mr_scr[0:1, :].to_broadcast([128, TOKT])), 16)
                dG.inc(gpsimd.dma_start(out=rB[:], in_=mr_scr[1:2, :].to_broadcast([128, TOKT])), 16)
            B.gpsimd(row_bcast)
            MRB_DONE = dG.n

            v_marks2 = []
            p_marks2 = []
            for fc in range(KC):
                sl = fc % 2

                def do_sub(vector, fc=fc, sl=sl):
                    if fc == 0:
                        vector.wait_ge(dG.h, MRB_DONE)
                    if fc >= 2:
                        vector.wait_ge(sV.h, p_marks2[fc - 2])
                    v_marks2.append(sV.inc(vector.tensor_sub(t1b[sl][:], xln[:, fc, :], mB[:])))
                B.vector(do_sub)

                def do_mul(vector, fc=fc, sl=sl):
                    vector.wait_ge(sV.h, v_marks2[fc])
                    p_marks2.append(sV.inc(vector.tensor_mul(xln[:, fc, :], t1b[sl][:], rB[:])))
                B.vector(do_mul)

            if debug:
                def dump_xln(gpsimd):
                    gpsimd.wait_ge(sV.h, p_marks2[-1])
                    for fc in range(KC):
                        dG.inc(gpsimd.dma_start(out=dbg["d_xln"][fc * 128:(fc + 1) * 128, :],
                                                in_=xln[:, fc, :]), 16)
                    gpsimd.wait_ge(dG.h, dG.n)
                B.gpsimd(dump_xln)

            cst.close()
            pq = [e0(nc.psum_tensor(f"pq{i}", [128, 512], F32)) for i in range(8)]

            blocks = [(DIN + 512 * i, 512) for i in range(8)]
            blocks.append((DIN + 4096, 320))
            blocks += [(512 * i, 512) for i in range(8)]

            def pad_init(vector):
                for i in range(2):
                    vector.memset(stg[i][:, 0:3], 0.0)
                    sV.inc(vector.memset(stg[i][:, 2051:2052], 0.0))
            B.vector(pad_init)

            pe_blk_marks = {}
            ot_marks = {}
            v_ot_marks = {}
            tile_idx = 0
            for bi, (of0, width) in enumerate(blocks):
                is_z = of0 < DIN
                sl = bi % 2
                ntok = TOK if is_z else TOKT
                toff = TOK if is_z else 0
                nts = ntok // 512

                def ld_w(gpsimd, bi=bi, of0=of0, width=width, sl=sl):
                    if bi >= 2:
                        gpsimd.wait_ge(sT.h, pe_blk_marks[bi - 2])
                    src = din["w1t"][:, of0:of0 + width].rearrange("(kc p) o -> p kc o", p=128)
                    dW[sl].inc(gpsimd.dma_start(out=wblk[sl][:, :, 0:width], in_=src), 16)
                    if bi == 1:
                        dsrc = din["convd"][:, :].rearrange("(t c p) m -> p t c m", p=128, t=5)
                        for tap in range(5):
                            for ch in range(2):
                                dG.inc(gpsimd.dma_start(
                                    out=diagw[:, tap, ch * 17:(ch + 1) * 17, :],
                                    in_=dsrc[:, tap, ch * 17:(ch + 1) * 17, :]), 16)
                B.gpsimd(ld_w)

                ntiles = (width + 127) // 128
                for ot in range(ntiles):
                    m = min(128, width - ot * 128)
                    of_abs = of0 + ot * 128
                    is_dt = of_abs >= 8448

                    def do_mm(tensor, bi=bi, ot=ot, m=m, sl=sl, nts=nts, toff=toff,
                              ti=tile_idx, is_last=(ot == ntiles - 1)):
                        if ot == 0:
                            tensor.wait_ge(dW[sl].h, dW[sl].n)
                        if ti >= 2:
                            tensor.wait_ge(sV.h, v_ot_marks[ti - 2])
                        pbase = 4 * (ti % 2)
                        for kc in range(KC):
                            if bi < 2:
                                tensor.wait_ge(sV.h, p_marks2[kc])
                            for ts in range(nts):
                                mm = tensor.matmul(
                                    pq[pbase + ts][0:m, :],
                                    wblk[sl][:, kc, ot * 128:ot * 128 + m],
                                    xln[:, kc, toff + ts * 512:toff + (ts + 1) * 512],
                                    start=(kc == 0), stop=(kc == KC - 1))
                        ot_marks[ti] = sT.inc(mm)
                        if is_last:
                            pe_blk_marks[bi] = ot_marks[ti]
                    B.tensor(do_mm)

                    def do_evac(vector, m=m, nts=nts, is_dt=is_dt, ti=tile_idx):
                        vector.wait_ge(sT.h, ot_marks[ti])
                        ssl = ti % 2
                        pbase = 4 * (ti % 2)
                        if is_dt:
                            for ts in range(nts):
                                last = vector.tensor_copy(
                                    g_dt_raw[0:NH, ts * 512:(ts + 1) * 512],
                                    pq[pbase + ts][0:NH, :])
                        else:
                            if ti >= 2:
                                vector.wait_ge(dS[ssl].h, dS[ssl].n)
                            for ts in range(nts):
                                last = vector.tensor_copy(
                                    stg[ssl][0:m, 3 + ts * 512:3 + (ts + 1) * 512],
                                    pq[pbase + ts][0:m, :])
                        v_ot_marks[ti] = sV.inc(last)
                    B.vector(do_evac)

                    if not is_dt:
                        def do_store(sync, m=m, of_abs=of_abs, is_z=is_z, ti=tile_idx):
                            sync.wait_ge(sV.h, v_ot_marks[ti])
                            ssl = ti % 2
                            if is_z:
                                dS[ssl].inc(sync.dma_start(
                                    out=z_scr[of_abs:of_abs + m, :],
                                    in_=stg[ssl][0:m, 3:3 + TOK]), 16)
                            else:
                                dS[ssl].inc(sync.dma_start(
                                    out=xbc_scr[of_abs - DIN:of_abs - DIN + m, 0:2052],
                                    in_=stg[ssl][0:m, 0:2052]), 16)
                        B.sync(do_store)
                    tile_idx += 1

            def p1_fin(sync):
                sync.wait_ge(dS[0].h, dS[0].n)
                sync.wait_ge(dS[1].h, dS[1].n)
            B.sync(p1_fin)

    c_xln.close()

    with contextlib.ExitStack() as c2:
        e2 = c2.enter_context
        b1 = e2(nc.sbuf_tensor("b1", [NH, TOKT], F32))
        b2 = e2(nc.sbuf_tensor("b2", [NH, TOKT], F32))
        b3 = e2(nc.sbuf_tensor("b3", [NH, TOKT], F32))
        b4 = e2(nc.sbuf_tensor("b4", [NH, TOKT], F32))
        anc = e2(nc.sbuf_tensor("anc", [NH, 2], F32))
        zrow = e2(nc.sbuf_tensor("zrow", [NH, CHK], F32))
        ptr2 = e2(nc.psum_tensor("ptr2", [128, NH], F32))

        with nc.Block() as B:
            def softplus_v(vector):
                m1 = sV.inc(vector.tensor_scalar(b1[:], g_dt_raw[:], g_dtb[:, 0:1], None, op0=ALU.add))
                vector.wait_ge(sV.h, m1)
                sV.inc(vector.tensor_scalar(b3[:], b1[:], 0.0, None, op0=ALU.max))
                sV.inc(vector.memset(zrow[:], 0.0))
            B.vector(softplus_v)
            SP1 = sV.n

            def softplus_s(scalar):
                scalar.wait_ge(sV.h, SP1)
                s0 = sS.inc(scalar.activation(b2[:], b1[:], AF.Abs))
                scalar.wait_ge(sS.h, s0)
                s1 = sS.inc(scalar.activation(b2[:], b2[:], AF.Exp, scale=np.float32(-1.0)))
                scalar.wait_ge(sS.h, s1)
                sS.inc(scalar.activation(b2[:], b2[:], AF.Ln, bias=np.float32(1.0)))
                sS.inc(scalar.activation(anc[:, 0:1], g_alog[:, 0:1], AF.Exp))
            B.scalar(softplus_s)
            SP2 = sS.n

            def dt_fin(vector):
                vector.wait_ge(sS.h, SP2)
                m1 = sV.inc(vector.tensor_add(b1[:], b2[:], b3[:]))
                sV.inc(vector.tensor_scalar(anc[:, 1:2], anc[:, 0:1], -1.0, None, op0=ALU.mult))
                vector.wait_ge(sV.h, sV.n)
                sV.inc(vector.tensor_scalar(b2[:], b1[:], anc[:, 1:2], None, op0=ALU.mult))
            B.vector(dt_fin)
            ADT_DONE = sV.n

            def ead_s(scalar):
                scalar.wait_ge(sV.h, ADT_DONE)
                sS.inc(scalar.activation(b3[:], b2[:], AF.Exp))
            B.scalar(ead_s)
            EAD_DONE = sS.n

            def ead_out(sync):
                sync.wait_ge(sS.h, EAD_DONE)
                dR.inc(sync.dma_start(out=ead_scr[:, :], in_=b3[:, :]), 16)
            B.sync(ead_out)
            EAD_ST = dR.n

            def acs_v(vector):
                vector.wait_ge(sV.h, ADT_DONE)
                for c in range(2 * NCH):
                    last = vector.tensor_tensor_scan(
                        b4[:, c * CHK:(c + 1) * CHK], b2[:, c * CHK:(c + 1) * CHK],
                        zrow[:, :], 0.0, op0=ALU.add, op1=ALU.add)
                sV.inc(last)
            B.vector(acs_v)
            ACS_DONE = sV.n

            def eac_s(scalar):
                scalar.wait_ge(sV.h, ACS_DONE)
                scalar.wait_ge(dR.h, EAD_ST)
                sS.inc(scalar.activation(b2[:], b4[:], AF.Exp))
            B.scalar(eac_s)
            EAC_DONE = sS.n

            def eac_out(sync):
                sync.wait_ge(sS.h, EAC_DONE)
                dR.inc(sync.dma_start(out=eac_scr[:, :], in_=b2[:, :]), 16)
                for c in range(2 * NCH):
                    dR.inc(sync.dma_start(out=cdr_scr[c:c + 1, :],
                                          in_=b2[:, c * CHK + CHK - 1:c * CHK + CHK],
                                          allow_slow_non_contiguous=True), 16)
            B.sync(eac_out)
            EAC_ST = dR.n

            def ds_v(vector):
                vector.wait_ge(dR.h, EAD_ST)
                vector.wait_ge(sV.h, ACS_DONE)
                for c in range(2 * NCH):
                    last = vector.tensor_scalar(
                        b3[:, c * CHK:(c + 1) * CHK], b4[:, c * CHK:(c + 1) * CHK],
                        b4[:, c * CHK + CHK - 1:c * CHK + CHK], None, op0=ALU.subtract)
                sV.inc(last)
            B.vector(ds_v)
            DSV_DONE = sV.n

            def ds_s(scalar):
                scalar.wait_ge(sV.h, DSV_DONE)
                sS.inc(scalar.activation(b3[:], b3[:], AF.Exp, scale=np.float32(-1.0)))
            B.scalar(ds_s)
            DS_DONE = sS.n

            tr_marks = []
            for arr, dst, wait_val, sem in [(b1, g_dtT, ADT_DONE, sV), (b3, g_dsT, DS_DONE, sS)]:
                for c in range(2 * NCH):
                    for sh in range(2):
                        def do_tr(tensor, arr=arr, c=c, sh=sh, wait_val=wait_val, sem=sem,
                                  first=(c == 0 and sh == 0)):
                            if first:
                                tensor.wait_ge(sem.h, wait_val)
                            if tr_marks:
                                tensor.wait_ge(sV.h, tr_marks[-1])
                            sT.inc(tensor.transpose(
                                ptr2[:, :], arr[0:NH, c * CHK + sh * 128:c * CHK + (sh + 1) * 128],
                                g_idf32[0:NH, 0:NH]))
                        B.tensor(do_tr)

                        def ev_tr(vector, dst=dst, c=c, sh=sh):
                            vector.wait_ge(sT.h, sT.n)
                            tr_marks.append(sV.inc(vector.tensor_copy(dst[:, c, sh, :], ptr2[:, :])))
                        B.vector(ev_tr)

            if debug:
                def dump_dt(gpsimd):
                    gpsimd.wait_ge(sV.h, sV.n)
                    gpsimd.wait_ge(sS.h, sS.n)
                    dG.inc(gpsimd.dma_start(out=dt_scr[:, :], in_=b1[:, :]), 16)
                    gpsimd.wait_ge(dG.h, dG.n)
                B.gpsimd(dump_dt)

            def st_dtT(sync):
                sync.wait_ge(sV.h, tr_marks[-1])
                dR.inc(sync.dma_start(out=dtT_scr[:, :],
                                      in_=g_dtT[:, :, :, :].rearrange("p a b c -> p (a b c)")), 16)
                dR.inc(sync.dma_start(out=dsT_scr[:, :],
                                      in_=g_dsT[:, :, :, :].rearrange("p a b c -> p (a b c)")), 16)
            B.sync(st_dtT)

            def p2_fin(sync):
                sync.wait_ge(dR.h, dR.n)
            B.sync(p2_fin)

    c_dtr.close()

    with contextlib.ExitStack() as c3:
        e3 = c3.enter_context
        state = e3(nc.sbuf_tensor("state", [128, NH * HD], F32))
        stateb = e3(nc.sbuf_tensor("stateb", [128, NH * HD], BF16))
        xbc_c = [e3(nc.sbuf_tensor(f"xbc_c{i}", [128, NCC * 260 + 8], BF16)) for i in range(2)]
        xconv = e3(nc.sbuf_tensor("xconv", [128, NCC * 260], BF16))
        tbz = e3(nc.sbuf_tensor("tbz", [128, 4096], BF16))
        xdtT = [e3(nc.sbuf_tensor(f"xdtT{i}", [128, NH * HD], BF16)) for i in range(2)]
        BT = [e3(nc.sbuf_tensor(f"BT{i}", [128, 128], BF16)) for i in range(2)]
        CBT2 = e3(nc.sbuf_tensor("CBT2", [128, 2, 384], BF16))
        C2 = e3(nc.sbuf_tensor("C2", [128, 512], BF16))
        Dsc = [e3(nc.sbuf_tensor(f"Dsc{i}", [128, 2, 384], BF16)) for i in range(2)]
        Csc = [e3(nc.sbuf_tensor(f"Csc{i}", [128, 2, CHK], BF16)) for i in range(2)]
        eadB = [e3(nc.sbuf_tensor(f"eadB{i}", [128, 2, CHK], F32)) for i in range(2)]
        eacB = [e3(nc.sbuf_tensor(f"eacB{i}", [128, 2, CHK], F32)) for i in range(2)]
        cdex = [e3(nc.sbuf_tensor(f"cdex{i}", [128, NH], F32)) for i in range(2)]
        xdtd = [e3(nc.sbuf_tensor(f"xdtd{i}", [128, 512], BF16)) for i in range(4)]
        stu = e3(nc.sbuf_tensor("stu", [128, 512], F32))
        y_c = e3(nc.sbuf_tensor("y_c", [128, 32, CHK], BF16))
        rB1 = e3(nc.sbuf_tensor("rB1", [128, CHK], F32))
        rr = e3(nc.sbuf_tensor("rr", [1, 768], F32))
        pCv = e3(nc.psum_tensor("pCv", [128, 1024], F32))
        pY = [e3(nc.psum_tensor(f"pY{i}", [128, 256], F32)) for i in range(2)]
        pT = e3(nc.psum_tensor("pT", [128, 1024], BF16))
        pCS = e3(nc.psum_tensor("pCS", [128, 512], F32))
        pE3 = e3(nc.psum_tensor("pE3", [1, 256], F32))

        xcv = xconv[:, :].rearrange("p (c t) -> p c t", t=260)

        with nc.Block() as B:
            def init_state(vector):
                vector.memset(state[:, :], 0.0)
                vector.memset(stateb[:, :], 0.0)
                vector.memset(xbc_c[0][:, NCC * 260:], 0.0)
                sV.inc(vector.memset(xbc_c[1][:, NCC * 260:], 0.0))
            B.vector(init_state)
            ST_INIT = sV.n

            WEXP = dG.n

            cm = {}
            pending_rms = []

            for c in range(2 * NCH):
                main = c >= NCH
                prev_main = c - 1 >= NCH
                tok0 = 3 + c * CHK
                mc = c - NCH

                def ld_chunk(sync, c=c, tok0=tok0, prev_main=prev_main):
                    if c == 0:
                        sync.wait_ge(sV.h, ST_INIT)
                    elif c >= 2:
                        sync.wait_ge(sT.h, cm[("cvmmq", c - 2, 16)])
                    if c >= 2:
                        sync.wait_ge(sV.h, cm[("supd", c - 2)])
                    src = xbc_scr[:, :].rearrange("(cc p) t -> p cc t", p=128)
                    dCp = dC if c % 2 == 0 else dC2
                    dCp.inc(sync.dma_start(
                        out=xbc_c[c % 2][:, 0:NCC * 260].rearrange("p (c t) -> p c t", t=260),
                        in_=src[:, :, tok0 - 3:tok0 + 257]), 16)
                    dCp.inc(sync.dma_start(out=cdex[c % 2][:, :],
                                          in_=cdr_scr[c:c + 1, :].to_broadcast([128, NH])), 16)
                B.sync(ld_chunk)
                CH_LD = (dC if c % 2 == 0 else dC2).n

                for q in range(17):
                    def conv_mm(tensor, c=c, q=q, CH_LD=CH_LD):
                        if q == 0:
                            tensor.wait_ge((dC if c % 2 == 0 else dC2).h, CH_LD)
                            if c == 0:
                                tensor.wait_ge(dG.h, WEXP)
                        if q >= 2:
                            tensor.wait_ge(sS.h, cm[("cvev", c, q - 2)])
                        elif c >= 1:
                            tensor.wait_ge(sS.h, cm[("cvev", c - 1, 15 + q)])
                            tensor.wait_ge(sV.h, cm[("cb_ev", c - 1)])
                        for ci in range(2):
                            cc = 2 * q + ci
                            po = pCv[:, (q % 2) * 512 + ci * 256:(q % 2) * 512 + (ci + 1) * 256]
                            for tap in range(4):
                                tensor.matmul(
                                    po, diagw[:, tap, cc, :],
                                    xbc_c[c % 2][:, cc * 260 + tap:cc * 260 + tap + 256],
                                    start=(tap == 0), stop=False)
                            mm = tensor.matmul(po, diagw[:, 4, cc, :], g_onesw[:, :],
                                               start=False, stop=True)
                        cm[("cvmm", c, 2 * q + 1)] = cm[("cvmmq", c, q)] = sT.inc(mm)
                    B.tensor(conv_mm)

                    def conv_ev(scalar, c=c, q=q, prev_main=prev_main):
                        scalar.wait_ge(sT.h, cm[("cvmmq", c, q)])
                        if q == 0 and c >= 1:
                            scalar.wait_ge(sT.h, cm[("pe_cb", c - 1)])
                            if prev_main:
                                scalar.wait_ge(sV.h, cm[("yev", c - 1, 31)])
                        cm[("cvev", c, q)] = sS.inc(scalar.activation(
                            xcv[:, 2 * q:2 * q + 2, 0:256],
                            pCv[:, (q % 2) * 512:(q % 2) * 512 + 512]
                            .rearrange("p (a b) -> p a b", a=2), AF.Silu))
                    B.scalar(conv_ev)
                cm[("silu", c)] = cm[("cvev", c, 16)]
                cm[("cvmm", c, NCC - 1)] = cm[("cvmmq", c, 16)]

                if debug and c == NCH:
                    def dump_xconv(gpsimd, c=c):
                        gpsimd.wait_ge(sS.h, cm[("silu", c)])
                        for g in range(NCC):
                            dG.inc(gpsimd.dma_start(out=dbg["d_xconv"][g * 128:(g + 1) * 128, :],
                                                    in_=xcv[:, g, 0:CHK]), 16)
                        gpsimd.wait_ge(dG.h, dG.n)
                    B.gpsimd(dump_xconv)

                def tr_bt(tensor, c=c):
                    tensor.wait_ge(sS.h, cm[("silu", c)])
                    if c >= 1:
                        tensor.wait_ge(sV.h, cm[("xdt_ev", c - 1, 1, 3)])
                    tensor.transpose(pT[:, 0:128], xcv[:, 32, 0:128], g_idbf[:, :])
                    cm[("pe_bt", c)] = sT.inc(tensor.transpose(
                        pT[:, 128:256], xcv[:, 32, 128:256], g_idbf[:, :]))
                B.tensor(tr_bt)

                def ev_bt(vector, c=c):
                    vector.wait_ge(sT.h, cm[("pe_bt", c)])
                    vector.tensor_copy(BT[0][:, :], pT[:, 0:128])
                    cm[("bt_ev", c)] = sV.inc(vector.tensor_copy(BT[1][:, :], pT[:, 128:256]))
                B.vector(ev_bt)

                for sh in range(2):
                    for g in range(4):
                        def tr_xdt(tensor, c=c, sh=sh, g=g):
                            if (sh, g) == (0, 0):
                                tensor.wait_ge(sV.h, cm[("bt_ev", c)])
                            else:
                                prev = (sh, g - 1) if g > 0 else (sh - 1, 3)
                                tensor.wait_ge(sV.h, cm[("xdt_ev", c, prev[0], prev[1])])
                            for f8 in range(8):
                                fc = g * 8 + f8
                                mm = tensor.transpose(pT[:, f8 * 128:(f8 + 1) * 128],
                                                      xcv[:, fc, sh * 128:sh * 128 + 128],
                                                      g_idbf[:, :])
                            cm[("pe_xdt", c, sh, g)] = sT.inc(mm)
                        B.tensor(tr_xdt)

                        def ev_xdt(vector, c=c, sh=sh, g=g, prev_main=prev_main):
                            vector.wait_ge(sT.h, cm[("pe_xdt", c, sh, g)])
                            if (sh, g) == (0, 0):
                                if c >= 1:
                                    vector.wait_ge(sP.h, cm[("xdtd", c - 1, 7)])
                            dtexp = _bc(g_dtT[:, c, sh, g * 16:(g + 1) * 16], 2, HD)
                            cm[("xdt_ev", c, sh, g)] = sV.inc(vector.tensor_mul(
                                xdtT[sh][:, g * 1024:(g + 1) * 1024].rearrange("p (h q) -> p h q", q=HD),
                                pT[:, 0:1024].rearrange("p (h q) -> p h q", q=HD),
                                dtexp))
                        B.vector(ev_xdt)

                def cb_mm(tensor, c=c):
                    if c >= 1:
                        tensor.wait_ge(sV.h, cm[("cb_ev", c - 1)])
                    tensor.matmul(pCv[:, 0:256], xcv[:, 32, 0:128], xcv[:, 33, 0:256],
                                  start=True, stop=True)
                    cm[("pe_cb", c)] = sT.inc(tensor.matmul(
                        pCv[:, 512:768], xcv[:, 32, 128:256], xcv[:, 33, 0:256],
                        start=True, stop=True))
                B.tensor(cb_mm)

                def cb_ev(vector, c=c):
                    vector.wait_ge(sT.h, cm[("pe_cb", c)])
                    if main and c > NCH:
                        vector.wait_ge(sP.h, cm[("m", c - 1, 31)])
                    vector.tensor_copy(CBT2[:, 0, 0:256], pCv[:, 0:256])
                    vector.tensor_copy(CBT2[:, 1, 0:256], pCv[:, 0:256])
                    vector.tensor_copy(CBT2[:, 0, 256:384], pCv[:, 640:768])
                    vector.tensor_copy(CBT2[:, 1, 256:384], pCv[:, 640:768])
                    vector.tensor_copy(C2[:, 0:256], xcv[:, 33, 0:256])
                    cm[("cb_ev", c)] = sV.inc(vector.tensor_copy(C2[:, 256:512], xcv[:, 33, 0:256]))
                B.vector(cb_ev)

                if pending_rms:
                    pending_rms.pop()()

                if main:
                    def emit_ld(j, c=c):
                        sl2 = j % 2

                        def ld_ead(sync, c=c, j=j, sl2=sl2):
                            if j >= 2:
                                sync.wait_ge(sV.h, cm[("scan", c, j - 2)])
                            elif c > NCH:
                                sync.wait_ge(sV.h, cm[("scan", c - 1, 30 + j)])
                            dE[sl2].inc(sync.dma_start(
                                out=eadB[sl2][:, :, :],
                                in_=ead_scr[2 * j:2 * j + 2, c * CHK:(c + 1) * CHK]
                                .unsqueeze(0).to_broadcast([128, 2, CHK])), 16)
                        B.sync(ld_ead)
                        cm[("eadld", c, j)] = dE[sl2].n

                        def ld_eac(sync, c=c, j=j, sl2=sl2):
                            if j >= 2:
                                sync.wait_ge(sP.h, cm[("m", c, j - 2)])
                            elif c > NCH:
                                sync.wait_ge(sP.h, cm[("m", c - 1, 30 + j)])
                            dEs[sl2].inc(sync.dma_start(
                                out=eacB[sl2][:, :, :],
                                in_=eac_scr[2 * j:2 * j + 2, c * CHK:(c + 1) * CHK]
                                .unsqueeze(0).to_broadcast([128, 2, CHK])), 16)
                        B.sync(ld_eac)
                        cm[("eacld", c, j)] = dEs[sl2].n

                    def emit_scan(j, c=c):
                        sl2 = j % 2

                        def do_scan(vector, c=c, j=j, sl2=sl2):
                            vector.wait_ge(dE[sl2].h, cm[("eadld", c, j)])
                            if j >= 2:
                                vector.wait_ge(sT.h, cm[("ymm", c, j - 2)])
                            elif c > NCH:
                                vector.wait_ge(sT.h, cm[("ymm", c - 1, 30 + j)])
                            for h2 in range(2):
                                vector.tensor_tensor_scan(
                                    Dsc[sl2][:, h2, 0:256], eadB[sl2][:, h2, :],
                                    g_dg0[:, :], 0.0, op0=ALU.mult, op1=ALU.max)
                                last = vector.tensor_tensor_scan(
                                    Dsc[sl2][:, h2, 256:384], eadB[sl2][:, h2, 128:256],
                                    g_dg1[:, :], 0.0, op0=ALU.mult, op1=ALU.max)
                            cm[("scan", c, j)] = sV.inc(last)
                        B.vector(do_scan)

                    def emit_m(j, c=c):
                        sl2 = j % 2

                        def do_m(pool, c=c, j=j, sl2=sl2):
                            pool.wait_ge(sV.h, cm[("scan", c, j)])
                            pool.wait_ge(dEs[sl2].h, cm[("eacld", c, j)])
                            if j <= 1:
                                pool.wait_ge(sV.h, cm[("cb_ev", c)])
                            if j >= 2:
                                pool.wait_ge(sT.h, cm[("ymm", c, j - 2)])
                            elif c > NCH:
                                pool.wait_ge(sT.h, cm[("ymm", c - 1, 30 + j)])
                            pool.tensor_mul(Dsc[sl2][:, :, :].rearrange("p a b -> p (a b)"),
                                            Dsc[sl2][:, :, :].rearrange("p a b -> p (a b)"),
                                            CBT2[:, :, :].rearrange("p a b -> p (a b)"))
                            cm[("m", c, j)] = sP.inc(pool.tensor_mul(
                                Csc[sl2][:, :, :].rearrange("p a b -> p (a b)"),
                                eacB[sl2][:, :, :].rearrange("p a b -> p (a b)"),
                                C2[:, :]))
                        B.gpsimd(do_m)

                    def emit_ymm(j, c=c):
                        sl2 = j % 2

                        def do_ymm(tensor, c=c, j=j, sl2=sl2):
                            tensor.wait_ge(sP.h, cm[("m", c, j)])
                            if j == 0:
                                tensor.wait_ge(sV.h, cm[("stb", c - 1)])
                            if j >= 2:
                                tensor.wait_ge(sV.h, cm[("yev", c, j - 2)])
                            elif c > NCH:
                                tensor.wait_ge(sV.h, cm[("yev", c - 1, 30 + j)])
                            for h2 in range(2):
                                h = 2 * j + h2
                                tp = (0, 64 * h2)
                                po = pY[sl2][h2 * 64:(h2 + 1) * 64, :]
                                tensor.matmul(po[:, 0:256],
                                              xdtT[0][:, h * HD:(h + 1) * HD],
                                              Dsc[sl2][:, h2, 0:256],
                                              start=True, stop=False, tile_position=tp)
                                tensor.matmul(po[:, 128:256],
                                              xdtT[1][:, h * HD:(h + 1) * HD],
                                              Dsc[sl2][:, h2, 256:384],
                                              start=False, stop=False, tile_position=tp)
                                mm = tensor.matmul(po[:, 0:256],
                                                   stateb[:, h * HD:(h + 1) * HD],
                                                   Csc[sl2][:, h2, :],
                                                   start=False, stop=True, tile_position=tp)
                            cm[("ymm", c, j)] = sT.inc(mm)
                        B.tensor(do_ymm)

                    def emit_yev(j, c=c):
                        sl2 = j % 2

                        def do_yev(vector, c=c, j=j, sl2=sl2):
                            vector.wait_ge(sT.h, cm[("ymm", c, j)])
                            if j == 0 and c > NCH:
                                vector.wait_ge(dY.h, cm[("yndma", c - 1)])
                            cm[("yev", c, j)] = sV.inc(vector.scalar_tensor_tensor(
                                y_c[:, j, :], xcv[:, j, 0:CHK], g_dpp[:, j:j + 1],
                                pY[sl2][:, :], op0=ALU.mult, op1=ALU.add))
                        B.vector(do_yev)

                    for st in range(34):
                        if st < 32:
                            emit_ld(st)
                        if 0 <= st - 1 < 32:
                            emit_m(st - 1)
                        if 0 <= st - 2 < 32:
                            emit_ymm(st - 2)
                        if st < 32:
                            emit_scan(st)
                        if 0 <= st - 2 < 32:
                            emit_yev(st - 2)

                for hg in range(8):
                    def do_xdtd(pool, c=c, hg=hg):
                        if hg == 0:
                            pool.wait_ge(sV.h, cm[("xdt_ev", c, 1, 3)])
                        elif hg >= 2:
                            pool.wait_ge(sT.h, cm[("cs_mm", c, hg - 2)])
                        par = 2 * (hg % 2)
                        for sh in range(2):
                            dsexp = _bc(g_dsT[:, c, sh, hg * 8:(hg + 1) * 8], 2, HD)
                            last = pool.tensor_mul(
                                xdtd[par + sh][:, :].rearrange("p (h q) -> p h q", q=HD),
                                xdtT[sh][:, hg * 512:(hg + 1) * 512].rearrange("p (h q) -> p h q", q=HD),
                                dsexp)
                        cm[("xdtd", c, hg)] = sP.inc(last)
                    B.gpsimd(do_xdtd)

                    def do_csmm(tensor, c=c, hg=hg, main=main):
                        tensor.wait_ge(sP.h, cm[("xdtd", c, hg)])
                        if hg == 0:
                            tensor.wait_ge(sV.h, cm[("bt_ev", c)])
                        if hg >= 1:
                            tensor.wait_ge(sV.h, cm[("supd_h", c, hg - 1)])
                        par = 2 * (hg % 2)
                        tensor.matmul(pCS[:, :], BT[0][:, :], xdtd[par][:, :],
                                      start=True, stop=False)
                        cm[("cs_mm", c, hg)] = sT.inc(tensor.matmul(
                            pCS[:, :], BT[1][:, :], xdtd[par + 1][:, :],
                            start=False, stop=True))
                    B.tensor(do_csmm)

                    def do_supd(vector, c=c, hg=hg):
                        vector.wait_ge(sT.h, cm[("cs_mm", c, hg)])
                        m1 = sV.inc(vector.tensor_mul(
                            stu[:, :].rearrange("p (h q) -> p h q", q=HD),
                            state[:, hg * 512:(hg + 1) * 512].rearrange("p (h q) -> p h q", q=HD),
                            _bc(cdex[c % 2][:, hg * 8:(hg + 1) * 8], 2, HD)))
                        vector.wait_ge(sV.h, m1)
                        cm[("supd_h", c, hg)] = sV.inc(vector.tensor_add(
                            state[:, hg * 512:(hg + 1) * 512], stu[:, :], pCS[:, :]))
                    B.vector(do_supd)
                cm[("supd", c)] = cm[("supd_h", c, 7)]

                def post_state(vector, c=c, main=main):
                    vector.wait_ge(sV.h, cm[("supd", c)])
                    if c == NCH - 1:
                        m1 = sV.inc(vector.tensor_scalar(
                            state[:, :], state[:, :], g_smask[:, 0:1], None, op0=ALU.mult))
                        vector.wait_ge(sV.h, m1)
                    if main:
                        vector.wait_ge(sT.h, cm[("ymm", c, 31)])
                    cm[("stb", c)] = sV.inc(vector.tensor_copy(stateb[:, :], state[:, :]))
                B.vector(post_state)

                if debug and c == NCH - 1:
                    def dump_state(gpsimd, c=c):
                        gpsimd.wait_ge(sV.h, cm[("stb", c)])
                        dG.inc(gpsimd.dma_start(out=dbg["d_state"][:, :], in_=state[:, :]), 16)
                        gpsimd.wait_ge(dG.h, dG.n)
                    B.gpsimd(dump_state)

                def emit_rms(c=c, mc=mc):
                    if debug:
                        def dump_y(gpsimd, c=c, mc=mc):
                            gpsimd.wait_ge(sV.h, cm[("yev", c, 31)])
                            for fc in range(32):
                                dG.inc(gpsimd.dma_start(
                                    out=dbg["d_y"][fc * 128:(fc + 1) * 128, mc * CHK:(mc + 1) * CHK],
                                    in_=y_c[:, fc, :]), 16)
                            gpsimd.wait_ge(dG.h, dG.n)
                            cm[("dumpy", c)] = dG.n
                        B.gpsimd(dump_y)

                    for hf in range(2):
                        z_h = tbz[:, 0:4096].rearrange("p (a b) -> p a b", a=16)
                        z_hf = tbz[:, 0:4096]

                        def ld_z(sync, c=c, hf=hf, mc=mc, z_h=z_h):
                            if hf == 0:
                                if c > NCH:
                                    sync.wait_ge(sP.h, cm[("yn", c - 1, 1)])
                            else:
                                sync.wait_ge(sT.h, cm[("sqmm", c, 0)])
                                sync.wait_ge(sV.h, cm[("y32", c, 0)])
                            zsrc = z_scr[:, :].rearrange("(fc p) t -> p fc t", p=128)
                            dZ.inc(sync.dma_start(
                                out=z_h,
                                in_=zsrc[:, hf * 16:(hf + 1) * 16, mc * CHK:(mc + 1) * CHK]), 16)
                        B.sync(ld_z)
                        ZLD = dZ.n

                        def z_silu(scalar, c=c, hf=hf, ZLD=ZLD, z_hf=z_hf):
                            scalar.wait_ge(dZ.h, ZLD)
                            cm[("zsilu", c, hf)] = sS.inc(scalar.activation(
                                z_hf, z_hf, AF.Silu))
                        B.scalar(z_silu)

                        def y32_mul(vector, c=c, hf=hf, z_hf=z_hf):
                            vector.wait_ge(sS.h, cm[("zsilu", c, hf)])
                            vector.wait_ge(sV.h, cm[("yev", c, 15 + 16 * hf)])
                            if ("dumpy", c) in cm:
                                vector.wait_ge(dG.h, cm[("dumpy", c)])
                            cm[("y32", c, hf)] = sV.inc(vector.tensor_mul(
                                y_c[:, hf * 16:(hf + 1) * 16, :].rearrange("p a b -> p (a b)"),
                                y_c[:, hf * 16:(hf + 1) * 16, :].rearrange("p a b -> p (a b)"),
                                z_hf))
                        B.vector(y32_mul)

                        def sq_mul(vector, c=c, hf=hf, z_hf=z_hf):
                            vector.wait_ge(sV.h, cm[("y32", c, hf)])
                            yflat = y_c[:, hf * 16:(hf + 1) * 16, :].rearrange(
                                "p a b -> p (a b)")
                            cm[("sq", c, hf)] = sV.inc(vector.tensor_mul(
                                z_hf, yflat, yflat))
                        B.vector(sq_mul)

                        def sq_mm(tensor, c=c, hf=hf, z_h=z_h):
                            tensor.wait_ge(sV.h, cm[("sq", c, hf)])
                            if c > NCH and hf == 0:
                                tensor.wait_ge(sV.h, cm[("rsq", c - 1)])
                            for f16 in range(16):
                                mm = tensor.matmul(pE3[:, :], g_ones[:, :],
                                                   z_h[:, f16, :],
                                                   start=(hf == 0 and f16 == 0),
                                                   stop=(hf == 1 and f16 == 15))
                            cm[("sqmm", c, hf)] = sT.inc(mm)
                        B.tensor(sq_mm)

                    def r_rsqrt(vector, c=c):
                        vector.wait_ge(sT.h, cm[("sqmm", c, 1)])
                        if c > NCH:
                            vector.wait_ge(dB.h, cm[("rb1", c - 1)])
                        v = rr[0:1, 0:256]
                        y = rr[0:1, 256:512]
                        t = rr[0:1, 512:768]
                        I32 = mybir.dt.int32
                        m = sV.inc(vector.tensor_scalar(v, pE3[0:1, :], 1.0 / DIN,
                                                        float(EPS), op0=ALU.mult, op1=ALU.add))
                        vector.wait_ge(sV.h, m)
                        m = sV.inc(vector.tensor_scalar(t.bitcast(I32), v.bitcast(I32), 1, None,
                                                        op0=ALU.arith_shift_right))
                        vector.wait_ge(sV.h, m)
                        m = sV.inc(vector.tensor_scalar(y.bitcast(I32), t.bitcast(I32), -1, 0x5f3759df,
                                                        op0=ALU.mult, op1=ALU.add))
                        for _ in range(2):
                            vector.wait_ge(sV.h, m)
                            m = sV.inc(vector.tensor_mul(t, y, y))
                            vector.wait_ge(sV.h, m)
                            m = sV.inc(vector.tensor_mul(t, t, v))
                            vector.wait_ge(sV.h, m)
                            m = sV.inc(vector.tensor_scalar(t, t, -0.5, 1.5,
                                                            op0=ALU.mult, op1=ALU.add))
                            vector.wait_ge(sV.h, m)
                            m = sV.inc(vector.tensor_mul(y, y, t))
                        cm[("rsq", c)] = m
                    B.vector(r_rsqrt)

                    def r_bounce(sync, c=c):
                        sync.wait_ge(sV.h, cm[("rsq", c)])
                        dR.inc(sync.dma_start(out=rr_scr[0:1, :], in_=rr[0:1, 256:512]), 16)
                        sync.wait_ge(dR.h, dR.n)
                        dB.inc(sync.dma_start(out=rB1[:, :],
                                              in_=rr_scr[0:1, :].to_broadcast([128, CHK])), 16)
                        cm[("rb1", c)] = dB.n
                    B.sync(r_bounce)

                    for hf in range(2):
                        ynt = tbz[:, 0:4096]

                        def yn_mul(pool, c=c, hf=hf, mc=mc, ynt=ynt):
                            pool.wait_ge(dB.h, cm[("rb1", c)])
                            pool.wait_ge(sT.h, cm[("sqmm", c, 1)])
                            if hf == 1:
                                pool.wait_ge(sP.h, cm[("yn", c, 0)])
                            wexp2 = _bc(g_normw[:, hf * 16:(hf + 1) * 16], 2, CHK)
                            m1 = sP.inc(pool.tensor_mul(
                                ynt.rearrange("p (a b) -> p a b", a=16),
                                y_c[:, hf * 16:(hf + 1) * 16, :], wexp2))
                            pool.wait_ge(sP.h, m1)
                            cm[("yn", c, hf)] = sP.inc(pool.tensor_mul(
                                y_c[:, hf * 16:(hf + 1) * 16, :],
                                ynt.rearrange("p (a b) -> p a b", a=16),
                                _bc(rB1[:, :], 1, 16)))
                        B.gpsimd(yn_mul)

                        def yn_st(sync, c=c, hf=hf, mc=mc):
                            sync.wait_ge(sP.h, cm[("yn", c, hf)])
                            dst = yn_scr[:, :].rearrange("(fc p) t -> p fc t", p=128)
                            dY.inc(sync.dma_start(
                                out=dst[:, hf * 16:(hf + 1) * 16, mc * CHK:(mc + 1) * CHK],
                                in_=y_c[:, hf * 16:(hf + 1) * 16, :]), 16)
                            cm[("yndma", c)] = dY.n
                        B.sync(yn_st)

                if main:
                    pending_rms.append(emit_rms)

            while pending_rms:
                pending_rms.pop()()

            def p3_fin(sync):
                sync.wait_ge(dY.h, dY.n)
                sync.wait_ge(sV.h, cm[("stb", 2 * NCH - 1)])
            B.sync(p3_fin)

    c_dtT.close()

    c45 = contextlib.ExitStack()
    g_xln2 = c45.enter_context(nc.sbuf_tensor("g_xln2", [128, KC, TOK], BF16, side="right"))

    with contextlib.ExitStack() as c4:
        e4 = c4.enter_context
        ynb = e4(nc.sbuf_tensor("ynb", [128, 32, TOK], BF16))
        w2og = [e4(nc.sbuf_tensor(f"w2og{i}", [128, 32, 512], BF16)) for i in range(2)]
        xres = [e4(nc.sbuf_tensor(f"xres{i}", [128, TOK], F32)) for i in range(4)]
        x2st = [e4(nc.sbuf_tensor(f"x2st{i}", [128, TOK], F32)) for i in range(4)]
        sqst = [e4(nc.sbuf_tensor("sqst0", [128, TOK], F32))]
        srow4 = e4(nc.sbuf_tensor("srow4", [1, 3 * 512], F32))

        with nc.Block() as B:
            p4m = {}
            with contextlib.ExitStack() as cA:
                pq4 = [cA.enter_context(nc.psum_tensor(f"pq4{i}", [128, 512], F32)) for i in range(4)]
                pst4 = [cA.enter_context(nc.psum_tensor(f"pst4{i}", [1, 512], F32)) for i in range(4)]

                ynld = []
                for kc in range(32):
                    def ld_yn(sync, kc=kc):
                        dX[0].inc(sync.dma_start(
                            out=ynb[:, kc, :],
                            in_=yn_scr[kc * 128:(kc + 1) * 128, :]), 16)
                        ynld.append(dX[0].n)
                    B.sync(ld_yn)

                for og in range(4):
                    sl = og % 2

                    def ld_w2(gpsimd, og=og, sl=sl):
                        if og >= 2:
                            gpsimd.wait_ge(sT.h, p4m[("mm", og - 2, 1)])
                        src = din["w2t"][:, og * 512:(og + 1) * 512].rearrange(
                            "(kc p) o -> p kc o", p=128)
                        dW[sl].inc(gpsimd.dma_start(out=w2og[sl][:, :, :], in_=src), 16)
                        p4m[("w2ld", og)] = dW[sl].n
                    B.gpsimd(ld_w2)

                    for ts in range(2):
                        def do_mm4(tensor, og=og, ts=ts, sl=sl):
                            tensor.wait_ge(dW[sl].h, p4m[("w2ld", og)])
                            if og == 0 and ts == 0:
                                tensor.wait_ge(dX[0].h, ynld[-1])
                            else:
                                pog, pts = (og, 0) if ts == 1 else (og - 1, 1)
                                for oi in range(4):
                                    tensor.wait_ge(sV.h, p4m[("x2", pog, oi, pts)])
                            for kc in range(32):
                                for oi in range(4):
                                    mm = tensor.matmul(
                                        pq4[oi][:, :],
                                        w2og[sl][:, kc, oi * 128:(oi + 1) * 128],
                                        ynb[:, kc, ts * 512:(ts + 1) * 512],
                                        start=(kc == 0), stop=(kc == 31))
                            p4m[("mm", og, ts)] = sT.inc(mm)
                        B.tensor(do_mm4)

                        if ts == 0:
                            for oi in range(4):
                                def ld_xres(sync, oi=oi, og=og, ot=og * 4 + oi):
                                    if og > 0:
                                        sync.wait_ge(sV.h, p4m[("x2", og - 1, oi, 1)])
                                    dC.inc(sync.dma_start(
                                        out=xres[oi][:, :],
                                        in_=din["xin"][ot * 128:(ot + 1) * 128, TOK:TOKT]), 16)
                                B.sync(ld_xres)
                            p4m[("xrl", og)] = dC.n

                        for oi in range(4):
                            ot = og * 4 + oi
                            xsl = oi % 2

                            def do_x2(vector, og=og, oi=oi, ts=ts, ot=ot, xsl=xsl):
                                vector.wait_ge(sT.h, p4m[("mm", og, ts)])
                                vector.wait_ge(dC.h, p4m[("xrl", og)])
                                if ts == 0 and ot >= 4:
                                    pot = ot - 4
                                    vector.wait_ge(dS[pot % 2].h, p4m[("x2dma", pot)])
                                    vector.wait_ge(sT.h, p4m[("stat", pot)])
                                p4m[("x2", og, oi, ts)] = sV.inc(vector.tensor_add(
                                    x2st[oi][:, ts * 512:(ts + 1) * 512], pq4[oi][:, :],
                                    xres[oi][:, ts * 512:(ts + 1) * 512]))
                            B.vector(do_x2)

                            if ts == 1:
                                def do_sq4(vector, og=og, oi=oi, ot=ot, xsl=xsl):
                                    vector.wait_ge(sV.h, p4m[("x2", og, oi, 1)])
                                    if ot >= 1:
                                        vector.wait_ge(sT.h, p4m[("stat", ot - 1)])
                                    p4m[("sq4", ot)] = sV.inc(vector.tensor_mul(
                                        sqst[0][:, :], x2st[oi][:, :], x2st[oi][:, :]))
                                B.vector(do_sq4)

                                def do_stat4(tensor, oi=oi, ot=ot, xsl=xsl):
                                    tensor.wait_ge(sV.h, p4m[("sq4", ot)])
                                    for ts2 in range(2):
                                        tensor.matmul(pst4[ts2][:, :], g_onesf[:, :],
                                                      x2st[oi][:, ts2 * 512:(ts2 + 1) * 512],
                                                      start=(ot == 0), stop=(ot == 15))
                                    for ts2 in range(2):
                                        mm = tensor.matmul(pst4[2 + ts2][:, :], g_onesf[:, :],
                                                           sqst[0][:, ts2 * 512:(ts2 + 1) * 512],
                                                           start=(ot == 0), stop=(ot == 15))
                                    p4m[("stat", ot)] = sT.inc(mm)
                                B.tensor(do_stat4)

                                def st_x2(sync, oi=oi, ot=ot):
                                    sync.wait_ge(sV.h, p4m[("sq4", ot)])
                                    if ot >= 2:
                                        sync.wait_ge(dS[ot % 2].h, p4m[("x2dma", ot - 2)])
                                    dS[ot % 2].inc(sync.dma_start(
                                        out=x2_scr[ot * 128:(ot + 1) * 128, :],
                                        in_=x2st[oi][:, :]), 16)
                                    p4m[("x2dma", ot)] = dS[ot % 2].n
                                B.sync(st_x2)

                for ts2 in range(2):
                    def fin4(vector, ts2=ts2):
                        if ts2 == 0:
                            vector.wait_ge(sT.h, p4m[("stat", 15)])
                        else:
                            vector.wait_ge(dR.h, dR.n)
                        vector.tensor_scalar(srow4[0:1, 0:512], pst4[ts2][:, :], 1.0 / D, None, op0=ALU.mult)
                        m1 = sV.inc(vector.tensor_scalar(srow4[0:1, 512:1024], pst4[2 + ts2][:, :],
                                                         1.0 / D, None, op0=ALU.mult))
                        vector.wait_ge(sV.h, m1)
                        m2 = sV.inc(vector.tensor_mul(srow4[0:1, 1024:1536], srow4[0:1, 0:512],
                                                      srow4[0:1, 0:512]))
                        vector.wait_ge(sV.h, m2)
                        sV.inc(vector.tensor_sub(srow4[0:1, 512:1024], srow4[0:1, 512:1024],
                                                 srow4[0:1, 1024:1536]))
                    B.vector(fin4)
                    VD4 = sV.n

                    def sq4(scalar, ts2=ts2, VD4=VD4):
                        scalar.wait_ge(sV.h, VD4)
                        sS.inc(scalar.activation(srow4[0:1, 1024:1536], srow4[0:1, 512:1024],
                                                 AF.Sqrt, bias=np.float32(EPS)))
                    B.scalar(sq4)
                    SD4 = sS.n

                    def rc4(vector, ts2=ts2, SD4=SD4):
                        vector.wait_ge(sS.h, SD4)
                        sV.inc(vector.reciprocal(srow4[0:1, 512:1024], srow4[0:1, 1024:1536]))
                    B.vector(rc4)
                    RD4 = sV.n

                    def st4(sync, ts2=ts2, RD4=RD4):
                        sync.wait_ge(sV.h, RD4)
                        dR.inc(sync.dma_start(out=r2_scr[0:1, ts2 * 512:(ts2 + 1) * 512],
                                              in_=srow4[0:1, 0:512]), 16)
                        dR.inc(sync.dma_start(out=r2_scr[1:2, ts2 * 512:(ts2 + 1) * 512],
                                              in_=srow4[0:1, 512:1024]), 16)
                        sync.wait_ge(dR.h, dR.n)
                    B.sync(st4)

                def p4_fin(sync):
                    sync.wait_ge(dS[0].h, dS[0].n)
                    sync.wait_ge(dS[1].h, dS[1].n)
                    sync.wait_ge(dR.h, dR.n)
                B.sync(p4_fin)

    with contextlib.ExitStack() as c5:
        e5 = c5.enter_context
        acc = e5(nc.sbuf_tensor("acc", [128, 16, TOK], F32))
        hbuf = [e5(nc.sbuf_tensor(f"hbuf{i}", [128, 6, TOK], BF16)) for i in range(2)]
        wgu = [e5(nc.sbuf_tensor(f"wgu{i}", [128, 16, 512], BF16)) for i in range(2)]
        cB5 = [e5(nc.sbuf_tensor(f"cB5{i}", [128, TOK], F32)) for i in range(2)]
        sgb = e5(nc.sbuf_tensor("sgb", [128, TOK], BF16))
        tgu = e5(nc.sbuf_tensor("tgu", [128, TOK], BF16))
        wrtb = e5(nc.sbuf_tensor("wrtb", [128, KC, NE], BF16))
        scores = e5(nc.sbuf_tensor("scores", [16, TOK], F32))
        scT = e5(nc.sbuf_tensor("scT", [128, 8, NE], F32))
        work = e5(nc.sbuf_tensor("work", [128, 8, NE], F32))
        selacc = e5(nc.sbuf_tensor("selacc", [128, 8, NE], F32))
        eqb = e5(nc.sbuf_tensor("eqb", [128, 8, NE], F32))
        redm = e5(nc.sbuf_tensor("redm", [128, 8, 2], F32))
        ebB = e5(nc.sbuf_tensor("ebB", [128, NE], F32))
        cmb16 = e5(nc.sbuf_tensor("cmb16", [16, TOK], F32))
        obuf = [e5(nc.sbuf_tensor(f"obuf{i}", [128, TOK], F32)) for i in range(2)]
        xr5 = [e5(nc.sbuf_tensor(f"xr5{i}", [128, TOK], F32)) for i in range(2)]
        x2l = xr5
        mB2 = e5(nc.sbuf_tensor("mB2", [128, TOK], F32))
        rB2 = e5(nc.sbuf_tensor("rB2", [128, TOK], F32))
        t2b = e5(nc.sbuf_tensor("t2b", [128, TOK], BF16))

        p5m = {}
        with nc.Block() as B:
            with contextlib.ExitStack() as cr:
                pr = cr.enter_context(nc.psum_tensor("pr", [16, TOK], F32))
                ptrA = cr.enter_context(nc.psum_tensor("ptrA", [128, NE], F32))
                ptrB = cr.enter_context(nc.psum_tensor("ptrB", [16, 128], F32))

                def ld_rt(gpsimd):
                    src = din["wrt"][:, :].rearrange("(kc p) e -> p kc e", p=128)
                    dRT.inc(gpsimd.dma_start(out=wrtb[:, :, :], in_=src), 16)
                B.gpsimd(ld_rt)

                def bc5(sync):
                    dB.inc(sync.dma_start(out=mB2[:, :],
                                          in_=r2_scr[0:1, :].to_broadcast([128, TOK])), 16)
                    dB.inc(sync.dma_start(out=rB2[:, :],
                                          in_=r2_scr[1:2, :].to_broadcast([128, TOK])), 16)
                B.sync(bc5)
                BC5 = dB.n

                for kc in range(KC):
                    csl = kc % 2

                    def ld_x2(sync, kc=kc, csl=csl):
                        if kc >= 2:
                            sync.wait_ge(sV.h, p5m[("sub", kc - 2)])
                        dX[csl].inc(sync.dma_start(out=x2l[csl][:, :],
                                                   in_=x2_scr[kc * 128:(kc + 1) * 128, :]), 16)
                        p5m[("x2ld", kc)] = dX[csl].n
                    B.sync(ld_x2)

                    def do_sub2(vector, kc=kc, csl=csl):
                        vector.wait_ge(dX[csl].h, p5m[("x2ld", kc)])
                        if kc == 0:
                            vector.wait_ge(dB.h, BC5)
                        if kc >= 1:
                            vector.wait_ge(sV.h, p5m[("ln2", kc - 1)])
                        p5m[("sub", kc)] = sV.inc(vector.tensor_sub(
                            t2b[:, :], x2l[csl][:, :], mB2[:, :]))
                    B.vector(do_sub2)

                    def do_mul2(vector, kc=kc):
                        vector.wait_ge(sV.h, p5m[("sub", kc)])
                        p5m[("ln2", kc)] = sV.inc(vector.tensor_mul(
                            g_xln2[:, kc, :], t2b[:, :], rB2[:, :]))
                    B.vector(do_mul2)

                if debug:
                    def dump_xln2(gpsimd):
                        gpsimd.wait_ge(sV.h, p5m[("ln2", KC - 1)])
                        for kc in range(KC):
                            dG.inc(gpsimd.dma_start(out=dbg["d_xln2"][kc * 128:(kc + 1) * 128, :],
                                                    in_=g_xln2[:, kc, :]), 16)
                        gpsimd.wait_ge(dG.h, dG.n)
                    B.gpsimd(dump_xln2)

                def ld_eb(sync):
                    dB.inc(sync.dma_start(out=ebB[:, :],
                                          in_=din["ebias"][0:1, :].to_broadcast([128, NE])), 16)
                B.sync(ld_eb)
                EB = dB.n

                def rt_mm(tensor):
                    tensor.wait_ge(dRT.h, dRT.n)
                    for kc in range(KC):
                        tensor.wait_ge(sV.h, p5m[("ln2", kc)])
                        for ts in range(2):
                            mm = tensor.matmul(pr[0:16, ts * 512:(ts + 1) * 512],
                                               wrtb[:, kc, :],
                                               g_xln2[:, kc, ts * 512:(ts + 1) * 512],
                                               start=(kc == 0), stop=(kc == KC - 1))
                    p5m["rtmm"] = sT.inc(mm)
                B.tensor(rt_mm)

                def rt_sig(scalar):
                    scalar.wait_ge(sT.h, p5m["rtmm"])
                    p5m["sig"] = sS.inc(scalar.activation(scores[:, :], pr[0:16, :], AF.Sigmoid))
                B.scalar(rt_sig)

                for tt in range(8):
                    def tr_sc(tensor, tt=tt):
                        if tt == 0:
                            tensor.wait_ge(sS.h, p5m["sig"])
                        else:
                            tensor.wait_ge(sV.h, p5m[("scev", tt - 1)])
                        p5m[("sctr", tt)] = sT.inc(tensor.transpose(
                            ptrA[:, :], scores[0:16, tt * 128:(tt + 1) * 128],
                            g_idf32[0:16, 0:16]))
                    B.tensor(tr_sc)

                    def ev_sc(vector, tt=tt):
                        vector.wait_ge(sT.h, p5m[("sctr", tt)])
                        p5m[("scev", tt)] = sV.inc(vector.tensor_copy(scT[:, tt, :], ptrA[:, :]))
                    B.vector(ev_sc)

                def topk(vector):
                    vector.wait_ge(sV.h, p5m[("scev", 7)])
                    vector.wait_ge(dB.h, EB)
                    m = sV.inc(vector.tensor_add(work[:, :, :], scT[:, :, :], _bc(ebB[:, :], 1, 8)))
                    vector.wait_ge(sV.h, m)
                    m = sV.inc(vector.memset(selacc[:, :, :], 0.0))
                    for k in range(4):
                        vector.wait_ge(sV.h, m)
                        m = sV.inc(vector.tensor_reduce(redm[:, :, 0:1], work[:, :, :],
                                                        axis=AX, op=ALU.max))
                        vector.wait_ge(sV.h, m)
                        m = sV.inc(vector.tensor_tensor(eqb[:, :, :], work[:, :, :],
                                                        redm[:, :, 0:1].to_broadcast([128, 8, NE]),
                                                        op=ALU.is_equal))
                        vector.wait_ge(sV.h, m)
                        m = sV.inc(vector.tensor_add(selacc[:, :, :], selacc[:, :, :], eqb[:, :, :]))
                        vector.wait_ge(sV.h, m)
                        m = sV.inc(vector.scalar_tensor_tensor(work[:, :, :], eqb[:, :, :],
                                                               -1e30, work[:, :, :],
                                                               op0=ALU.mult, op1=ALU.add))
                    vector.wait_ge(sV.h, m)
                    m = sV.inc(vector.tensor_mul(work[:, :, :], scT[:, :, :], selacc[:, :, :]))
                    vector.wait_ge(sV.h, m)
                    m = sV.inc(vector.tensor_reduce(redm[:, :, 0:1], work[:, :, :],
                                                    axis=AX, op=ALU.add))
                    vector.wait_ge(sV.h, m)
                    m = sV.inc(vector.tensor_scalar(redm[:, :, 1:2], redm[:, :, 0:1],
                                                    1e-20, None, op0=ALU.add))
                    vector.wait_ge(sV.h, m)
                    m = sV.inc(vector.reciprocal(redm[:, :, 0:1], redm[:, :, 1:2]))
                    vector.wait_ge(sV.h, m)
                    p5m["cmbT"] = sV.inc(vector.tensor_mul(
                        work[:, :, :], work[:, :, :],
                        redm[:, :, 0:1].to_broadcast([128, 8, NE])))
                B.vector(topk)

                for tt in range(8):
                    def tr_cm(tensor, tt=tt):
                        if tt == 0:
                            tensor.wait_ge(sV.h, p5m["cmbT"])
                        else:
                            tensor.wait_ge(sV.h, p5m[("cmev", tt - 1)])
                        p5m[("cmtr", tt)] = sT.inc(tensor.transpose(
                            ptrB[:, :], work[:, tt, :], g_idf32[:, :]))
                    B.tensor(tr_cm)

                    def ev_cm(vector, tt=tt):
                        vector.wait_ge(sT.h, p5m[("cmtr", tt)])
                        p5m[("cmev", tt)] = sV.inc(vector.tensor_copy(
                            cmb16[0:16, tt * 128:(tt + 1) * 128], ptrB[0:16, :]))
                    B.vector(ev_cm)

                def st_cmb(sync):
                    sync.wait_ge(sV.h, p5m[("cmev", 7)])
                    dR.inc(sync.dma_start(out=cmb_scr[:, :], in_=cmb16[:, :]), 16)
                    sync.wait_ge(dR.h, dR.n)
                B.sync(st_cmb)

                if debug:
                    def dump_rt(gpsimd):
                        gpsimd.wait_ge(sV.h, p5m[("cmev", 7)])
                        dG.inc(gpsimd.dma_start(out=dbg["d_scores"][:, :], in_=scores[:, :]), 16)
                        dG.inc(gpsimd.dma_start(out=dbg["d_cmb"][:, :], in_=cmb16[:, :]), 16)
                        gpsimd.wait_ge(dG.h, dG.n)
                    B.gpsimd(dump_rt)

                def r_fin(sync):
                    sync.wait_ge(sV.h, p5m[("cmev", 7)])
                    sync.wait_ge(sS.h, sS.n)
                B.sync(r_fin)

            pg = e5(nc.psum_tensor("pg", [128, 1024], F32))
            pu = e5(nc.psum_tensor("pu", [128, 1024], F32))
            pdt = [e5(nc.psum_tensor(f"pdt{i}", [128, 1024], F32)) for i in range(2)]

            NEXP = NE + 1
            for grp in range(9):
                exps = [2 * grp, 2 * grp + 1] if grp < 8 else [NE]
                for ei, e in enumerate(exps):
                    shared = e == NE
                    def ld_cb(sync, e=e, ei=ei, grp=grp, shared=shared):
                        if not shared:
                            if grp > 0:
                                sync.wait_ge(sV.h, p5m[("h", 2 * (grp - 1) + ei)])
                            dE[ei].inc(sync.dma_start(
                                out=cB5[ei][:, :],
                                in_=cmb_scr[e:e + 1, :].to_broadcast([128, TOK])), 16)
                        p5m[("cb", e)] = (ei, dE[ei].n)
                    B.sync(ld_cb)

                    for blk in range(3):
                        wsl = (3 * e + blk) % 2

                        def ld_gu(gpsimd, e=e, ei=ei, blk=blk, wsl=wsl, shared=shared, grp=grp):
                            gi = 3 * e + blk
                            if gi >= 2:
                                gpsimd.wait_ge(sT.h, p5m[("gumm", gi - 2)])
                            if shared:
                                src = din["sgut"][:, blk * 512:(blk + 1) * 512]
                            else:
                                src = din["gut"][e, :, blk * 512:(blk + 1) * 512]
                            srcr = src.rearrange("(kc p) o -> p kc o", p=128)
                            if ei == 0 and blk <= 1 and grp > 0:
                                dGA[wsl].inc(gpsimd.dma_start(
                                    out=wgu[wsl][:, 12:16, :], in_=srcr[:, 12:16, :]), 16)
                                gpsimd.wait_ge(sT.h, p5m[("dnmm", grp - 1, 2, 3)])
                                dGB[wsl].inc(gpsimd.dma_start(
                                    out=wgu[wsl][:, 0:6, :], in_=srcr[:, 0:6, :]), 16)
                                gpsimd.wait_ge(sT.h, p5m[("dnmm", grp - 1, 3, 3)])
                                dW[wsl].inc(gpsimd.dma_start(
                                    out=wgu[wsl][:, 6:12, :], in_=srcr[:, 6:12, :]), 16)
                                p5m[("gusplit", e, blk)] = (dGA[wsl].n, dGB[wsl].n)
                            else:
                                dW[wsl].inc(gpsimd.dma_start(
                                    out=wgu[wsl][:, :, :], in_=srcr), 16)
                        B.gpsimd(ld_gu)

                        for jj in range(2):
                            jfull = blk * 2 + jj

                            def gu_mm(tensor, e=e, blk=blk, jj=jj, wsl=wsl, jfull=jfull):
                                gi = 3 * e + blk
                                if jj == 0:
                                    tensor.wait_ge(dW[wsl].h, dW[wsl].n)
                                    if ("gusplit", e, blk) in p5m:
                                        na, nb = p5m[("gusplit", e, blk)]
                                        tensor.wait_ge(dGA[wsl].h, na)
                                        tensor.wait_ge(dGB[wsl].h, nb)
                                if e == 0 and blk == 0 and jj == 0:
                                    tensor.wait_ge(sV.h, p5m[("cmev", 7)])
                                    tensor.wait_ge(sV.h, p5m[("ln2", KC - 1)])
                                for ts in range(2):
                                    kg = ("gsh", e, jfull - 1, ts) if jfull > 0 else \
                                         (("gsh", e - 1, 5, ts) if e > 0 else None)
                                    if kg and kg in p5m:
                                        tensor.wait_ge(sS.h, p5m[kg])
                                    ku = ("guevh", e, jfull - 1, ts) if jfull > 0 else \
                                         (("guevh", e - 1, 5, ts) if e > 0 else None)
                                    if ku and ku in p5m:
                                        tensor.wait_ge(sV.h, p5m[ku])
                                    for kc in range(KC):
                                        tensor.matmul(pg[:, ts * 512:(ts + 1) * 512],
                                                      wgu[wsl][:, kc, jj * 256:jj * 256 + 128],
                                                      g_xln2[:, kc, ts * 512:(ts + 1) * 512],
                                                      start=(kc == 0), stop=(kc == KC - 1))
                                    for kc in range(KC):
                                        mm = tensor.matmul(pu[:, ts * 512:(ts + 1) * 512],
                                                           wgu[wsl][:, kc, jj * 256 + 128:(jj + 1) * 256],
                                                           g_xln2[:, kc, ts * 512:(ts + 1) * 512],
                                                           start=(kc == 0), stop=(kc == KC - 1))
                                    p5m[("gummh", e, jfull, ts)] = sT.inc(mm)
                                p5m[("gumm", 3 * e + blk)] = p5m[("gummj", e, jfull)] = \
                                    p5m[("gummh", e, jfull, 1)]
                            B.tensor(gu_mm)

                            for ts in range(2):
                                def gu_silu(scalar, e=e, jfull=jfull, ts=ts):
                                    scalar.wait_ge(sT.h, p5m[("gummh", e, jfull, ts)])
                                    key = ("guevh", e, jfull - 1, ts) if jfull > 0 else \
                                          (("guevh", e - 1, 5, ts) if e > 0 else None)
                                    if key and key in p5m:
                                        scalar.wait_ge(sV.h, p5m[key])
                                    p5m[("gsh", e, jfull, ts)] = sS.inc(scalar.activation(
                                        sgb[:, ts * 512:(ts + 1) * 512],
                                        pg[:, ts * 512:(ts + 1) * 512], AF.Silu))
                                B.scalar(gu_silu)

                                def gu_mul(vector, e=e, jfull=jfull, ts=ts):
                                    vector.wait_ge(sS.h, p5m[("gsh", e, jfull, ts)])
                                    key = ("hjh", e, jfull - 1, ts) if jfull > 0 else \
                                          (("hjh", e - 1, 5, ts) if e > 0 else None)
                                    if key and key in p5m:
                                        vector.wait_ge(sV.h, p5m[key])
                                    p5m[("guevh", e, jfull, ts)] = sV.inc(vector.tensor_mul(
                                        tgu[:, ts * 512:(ts + 1) * 512],
                                        sgb[:, ts * 512:(ts + 1) * 512],
                                        pu[:, ts * 512:(ts + 1) * 512]))
                                B.vector(gu_mul)

                                def h_mul(vector, e=e, ei=ei, jfull=jfull, ts=ts, shared=shared):
                                    vector.wait_ge(sV.h, p5m[("guevh", e, jfull, ts)])
                                    if not shared:
                                        cbi, cbv = p5m[("cb", e)]
                                        vector.wait_ge(dE[cbi].h, cbv)
                                        p5m[("hjh", e, jfull, ts)] = sV.inc(vector.tensor_mul(
                                            hbuf[ei][:, jfull, ts * 512:(ts + 1) * 512],
                                            tgu[:, ts * 512:(ts + 1) * 512],
                                            cB5[ei][:, ts * 512:(ts + 1) * 512]))
                                    else:
                                        p5m[("hjh", e, jfull, ts)] = sV.inc(vector.tensor_copy(
                                            hbuf[ei][:, jfull, ts * 512:(ts + 1) * 512],
                                            tgu[:, ts * 512:(ts + 1) * 512]))
                                B.vector(h_mul)
                            p5m[("hj", e, jfull)] = p5m[("hjh", e, jfull, 1)]
                    p5m[("h", e)] = p5m[("hj", e, 5)]

                nexp_g = len(exps)
                for ob in range(4):
                    for ei, e in enumerate(exps):
                        wsl = ei

                        def ld_dn(gpsimd, e=e, ei=ei, ob=ob, grp=grp, wsl=wsl):
                            if ob <= 1:
                                for e2 in exps:
                                    gpsimd.wait_ge(sT.h, p5m[("gumm", 3 * e2 + 2)])
                            else:
                                gpsimd.wait_ge(sT.h, p5m[("dnmm", grp, ob - 2, 3)])
                            if e == NE:
                                src = din["sdt"][:, ob * 512:(ob + 1) * 512]
                            else:
                                src = din["dwt"][e, :, ob * 512:(ob + 1) * 512]
                            half = 6 * (ob % 2)
                            sem = dW[wsl] if ob % 2 == 0 else dW2[wsl]
                            sem.inc(gpsimd.dma_start(
                                out=wgu[wsl][:, half:half + 6, :],
                                in_=src.rearrange("(kc p) o -> p kc o", p=128)), 16)
                            p5m[("dnld", grp, ob, wsl)] = sem.n
                        B.gpsimd(ld_dn)

                    for ot4 in range(4):
                        ot = ob * 4 + ot4

                        def dn_mm(tensor, grp=grp, ob=ob, ot4=ot4, exps=exps):
                            if ot4 == 0:
                                for wsl in range(len(exps)):
                                    sem = dW[wsl] if ob % 2 == 0 else dW2[wsl]
                                    tensor.wait_ge(sem.h, p5m[("dnld", grp, ob, wsl)])
                                if ob == 0:
                                    for e2 in exps:
                                        tensor.wait_ge(sV.h, p5m[("h", e2)])
                            key = ("dnev", grp, ob, ot4 - 2) if ot4 >= 2 else \
                                  (("dnev", grp, ob - 1, 2 + ot4) if ob > 0 else
                                   (("dnev", grp - 1, 3, 2 + ot4) if grp > 0 else None))
                            if key and key in p5m:
                                tensor.wait_ge(sV.h, p5m[key])
                            pdx = pdt[ot4 % 2]
                            half = 6 * (ob % 2)
                            for ei in range(len(exps)):
                                for kc in range(6):
                                    for ts in range(2):
                                        mm = tensor.matmul(
                                            pdx[:, ts * 512:(ts + 1) * 512],
                                            wgu[ei][:, half + kc, ot4 * 128:(ot4 + 1) * 128],
                                            hbuf[ei][:, kc, ts * 512:(ts + 1) * 512],
                                            start=(ei == 0 and kc == 0),
                                            stop=(ei == len(exps) - 1 and kc == 5))
                            p5m[("dnmm", grp, ob, ot4)] = sT.inc(mm)
                        B.tensor(dn_mm)

                        if grp == 8:
                            osl = ot % 2

                            def ld_xr5(sync, ot=ot, osl=osl):
                                if ot >= 2:
                                    sync.wait_ge(dO.h, dO.n)
                                else:
                                    sync.wait_ge(sV.h, p5m[("sub", KC - 1)])
                                dX[osl].inc(sync.dma_start(
                                    out=xr5[osl][:, :],
                                    in_=x2_scr[ot * 128:(ot + 1) * 128, :]), 16)
                            B.sync(ld_xr5)

                            def fin_ev(vector, grp=grp, ob=ob, ot4=ot4, ot=ot, osl=osl):
                                vector.wait_ge(sT.h, p5m[("dnmm", grp, ob, ot4)])
                                vector.wait_ge(dX[osl].h, dX[osl].n)
                                if ot >= 2:
                                    vector.wait_ge(dO.h, dO.n)
                                pdx = pdt[ot4 % 2]
                                m1 = sV.inc(vector.scalar_tensor_tensor(
                                    obuf[osl][:, :], pdx[:, :], 1.0,
                                    acc[:, ot, :], op0=ALU.mult, op1=ALU.add))
                                vector.wait_ge(sV.h, m1)
                                m1 = sV.inc(vector.tensor_add(
                                    obuf[osl][:, :], obuf[osl][:, :], xr5[osl][:, :]))
                                p5m[("dnev", grp, ob, ot4)] = m1
                            B.vector(fin_ev)

                            def st_out(sync, ot=ot, osl=osl, grp=grp, ob=ob, ot4=ot4):
                                sync.wait_ge(sV.h, p5m[("dnev", grp, ob, ot4)])
                                dO.inc(sync.dma_start(out=out_hbm[ot * 128:(ot + 1) * 128, :],
                                                      in_=obuf[osl][:, :]), 16)
                            B.sync(st_out)
                        else:
                            def dn_ev(vector, grp=grp, ob=ob, ot4=ot4, ot=ot):
                                vector.wait_ge(sT.h, p5m[("dnmm", grp, ob, ot4)])
                                pdx = pdt[ot4 % 2]
                                if grp == 0:
                                    p5m[("dnev", grp, ob, ot4)] = sV.inc(
                                        vector.tensor_copy(acc[:, ot, :], pdx[:, :]))
                                else:
                                    p5m[("dnev", grp, ob, ot4)] = sV.inc(
                                        vector.scalar_tensor_tensor(
                                            acc[:, ot, :], pdx[:, :], 1.0,
                                            acc[:, ot, :], op0=ALU.mult, op1=ALU.add))
                            B.vector(dn_ev)

            def p5_fin(sync):
                sync.wait_ge(dO.h, dO.n)
            B.sync(p5_fin)

    c45.close()
    ctx.close()

    return nc, dbg



def _prep_shared(L0_attn_in_proj_w, L0_attn_conv_w, L0_attn_conv_b, L0_attn_dt_bias,
                 L0_attn_A_log, L0_attn_D, L0_attn_norm_w, L0_attn_out_proj_w,
                 L0_ffn_router_w, L0_ffn_e_bias, L0_ffn_gate_up_proj, L0_ffn_down_proj,
                 L0_ffn_shared_gate_w, L0_ffn_shared_up_w, L0_ffn_shared_down_w):
    C = np.ascontiguousarray
    A = np.asarray
    sh = {}
    sh["w1t"] = C(A(L0_attn_in_proj_w).T)
    sh["convw"] = C(A(L0_attn_conv_w).reshape(NCC, 128, 4).transpose(1, 0, 2).reshape(128, NCC * 4))
    sh["convb"] = C(A(L0_attn_conv_b).reshape(NCC, 128).T)
    cw = A(L0_attn_conv_w).reshape(NCC, 128, 4)
    cb5 = A(L0_attn_conv_b).reshape(NCC, 128)
    d4 = np.zeros((5, NCC, 128, 128), np.float32)
    ii = np.arange(128)
    for tap in range(4):
        for cc in range(NCC):
            d4[tap, cc, ii, ii] = cw[cc, :, tap]
    for cc in range(NCC):
        d4[4, cc, ii, ii] = cb5[cc]
    sh["convd"] = C(d4.reshape(5 * NCC * 128, 128))
    sh["dtb"] = C(A(L0_attn_dt_bias).reshape(NH, 1))
    sh["alog"] = C(A(L0_attn_A_log).reshape(NH, 1))
    sh["dpp"] = C(np.repeat(A(L0_attn_D), HD).reshape(32, 128).T)
    sh["normw"] = C(A(L0_attn_norm_w).reshape(32, 128).T)
    sh["w2t"] = C(A(L0_attn_out_proj_w).T)
    sh["wrt"] = C(A(L0_ffn_router_w).T)
    sh["ebias"] = C(A(L0_ffn_e_bias).reshape(1, NE))
    gu = A(L0_ffn_gate_up_proj)
    g = gu[:, :IE, :]
    u = gu[:, IE:, :]
    inter = np.empty((NE, 2 * IE, D), np.float32)
    for j in range(IE // 128):
        inter[:, j * 256:j * 256 + 128, :] = g[:, j * 128:(j + 1) * 128, :]
        inter[:, j * 256 + 128:(j + 1) * 256, :] = u[:, j * 128:(j + 1) * 128, :]
    sh["gut"] = C(inter.transpose(0, 2, 1))
    sh["dwt"] = C(A(L0_ffn_down_proj).transpose(0, 2, 1))
    sg = A(L0_ffn_shared_gate_w)
    su = A(L0_ffn_shared_up_w)
    sinter = np.empty((2 * IE, D), np.float32)
    for j in range(IE // 128):
        sinter[j * 256:j * 256 + 128, :] = sg[j * 128:(j + 1) * 128, :]
        sinter[j * 256 + 128:(j + 1) * 256, :] = su[j * 128:(j + 1) * 128, :]
    sh["sgut"] = C(sinter.T)
    sh["sdt"] = C(A(L0_ffn_shared_down_w).T)
    return sh


def _core_inputs(x_full, sh):
    maps = []
    for core in range(8):
        b, h = core // 2, core % 2
        xm = np.ascontiguousarray(x_full[b, h * TOK:(h + 1) * TOK, :].T)
        if h == 0:
            xh = np.zeros((D, TOK), np.float32)
            sm = np.zeros((128, 1), np.float32)
        else:
            xh = np.ascontiguousarray(x_full[b, 0:TOK, :].T)
            sm = np.ones((128, 1), np.float32)
        m = dict(sh)
        m["xin"] = np.ascontiguousarray(np.concatenate([xh, xm], axis=1))
        m["smask"] = sm
        maps.append(m)
    return maps


def kernel(L0_attn_hidden_states, **kw):
    x = np.asarray(L0_attn_hidden_states)
    sh = _prep_shared(**kw)
    nc, _ = build(debug=False)
    maps = _core_inputs(x, sh)
    res = run_bass_kernel_spmd(nc, maps, core_ids=list(range(8)))
    out = np.empty((4, 2048, 2048), np.float32)
    for core in range(8):
        b, h = core // 2, core % 2
        out[b, h * TOK:(h + 1) * TOK, :] = res.results[core]["out"].T
    return out


# version 23
# speedup vs baseline: 1.0228x; 1.0228x over previous
import contextlib
import numpy as np
import ml_dtypes
import concourse.bass as bass
import concourse.mybir as mybir
from concourse.bass_utils import run_bass_kernel_spmd

F32 = mybir.dt.float32
BF16 = mybir.dt.bfloat16
AF = mybir.ActivationFunctionType
ALU = mybir.AluOpType
AX = mybir.AxisListType.X
bf16 = ml_dtypes.bfloat16

D = 2048
KC = 16
DIN = 4096
DST = 128
NH = 64
HD = 64
CHK = 256
CONVD = DIN + 2 * DST
NCC = CONVD // 128
PROJ = 8512
TOK = 1024
TOKT = 2048
NCH = 4
EPS = 1e-5
NE = 16
IE = 768


class Sem:
    def __init__(self, ctx, nc, name):
        self.h = ctx.enter_context(nc.semaphore(name))
        self.n = 0

    def inc(self, instr, k=1):
        instr.then_inc(self.h, k)
        self.n += k
        return self.n


def _bc(ap, axis, n):
    shp = list(ap.shape)
    shp.insert(axis, n)
    return ap.unsqueeze(axis).to_broadcast(shp)


def build(debug=False):
    nc = bass.Bass("TRN2", target_bir_lowering=False)

    din = {}
    for name, shp in [
        ("xin", [D, TOKT]), ("smask", [128, 1]), ("w1t", [D, PROJ]),
        ("convw", [128, NCC * 4]), ("convb", [128, NCC]),
        ("convd", [5 * NCC * 128, 128]),
        ("dtb", [NH, 1]), ("alog", [NH, 1]), ("dpp", [128, 32]),
        ("normw", [128, 32]), ("w2t", [DIN, D]), ("wrt", [D, NE]),
        ("ebias", [1, NE]), ("gut", [NE, D, 2 * IE]), ("dwt", [NE, IE, D]),
        ("sgut", [D, 2 * IE]), ("sdt", [IE, D]),
    ]:
        din[name] = nc.dram_tensor(name, shp, F32, kind="ExternalInput").ap()

    out_hbm = nc.dram_tensor("out", [D, TOK], F32, kind="ExternalOutput").ap()

    skind = "ExternalOutput" if debug else "Internal"
    xbc_scr = nc.dram_tensor("xbc_scr", [CONVD, 4 + TOKT], BF16, kind=skind).ap()
    z_scr = nc.dram_tensor("z_scr", [DIN, TOK], BF16, kind=skind).ap()
    x2_scr = nc.dram_tensor("x2_scr", [D, TOK], F32, kind=skind).ap()
    yn_scr = nc.dram_tensor("yn_scr", [DIN, TOK], BF16, kind=skind).ap()
    ead_scr = nc.dram_tensor("ead_scr", [NH, TOKT], F32, kind=skind).ap()
    eac_scr = nc.dram_tensor("eac_scr", [NH, TOKT], F32, kind=skind).ap()
    cdr_scr = nc.dram_tensor("cdr_scr", [2 * NCH, NH], F32, kind=skind).ap()
    dt_scr = nc.dram_tensor("dt_scr", [NH, TOKT], F32, kind=skind).ap() if debug else None
    mr_scr = nc.dram_tensor("mr_scr", [2, TOKT], F32, kind="Internal").ap()
    dtT_scr = nc.dram_tensor("dtT_scr", [128, 2 * NCH * 2 * NH], F32, kind="Internal").ap()
    dsT_scr = nc.dram_tensor("dsT_scr", [128, 2 * NCH * 2 * NH], F32, kind="Internal").ap()
    r2_scr = nc.dram_tensor("r2_scr", [2, TOK], F32, kind="Internal").ap()
    rr_scr = nc.dram_tensor("rr_scr", [1, CHK], F32, kind="Internal").ap()
    cmb_scr = nc.dram_tensor("cmb_scr", [NE, TOK], F32, kind="Internal").ap()
    dbg = {}
    if debug:
        for name, shp in [("d_xln", [D, TOKT]), ("d_state", [128, NH * HD]),
                          ("d_y", [DIN, TOK]),
                          ("d_xln2", [D, TOK]), ("d_cmb", [NE, TOK]),
                          ("d_scores", [NE, TOK]), ("d_xconv", [CONVD, CHK])]:
            dbg[name] = nc.dram_tensor(name, shp, F32, kind="ExternalOutput").ap()
        dbg["dt_scr"] = dt_scr

    ones_col = nc.inline_tensor(np.ones((128, 1), dtype=bf16), "ones_col").ap()
    ones_col_f = nc.inline_tensor(np.ones((128, 1), dtype=np.float32), "ones_col_f").ap()
    ones_w = nc.inline_tensor(np.ones((128, 256), dtype=bf16), "ones_w").ap()
    id_bf = nc.inline_tensor(np.eye(128, dtype=bf16), "id_bf").ap()
    id_f32 = nc.inline_tensor(np.eye(128, dtype=np.float32), "id_f32").ap()
    dg0_c = nc.inline_tensor(np.eye(128, 256, dtype=bf16), "dg0").ap()
    dg1_c = nc.inline_tensor(np.eye(128, 128, dtype=bf16), "dg1").ap()

    ctx = contextlib.ExitStack()
    E = ctx.enter_context

    sT = Sem(ctx, nc, "sT")
    sV = Sem(ctx, nc, "sV")
    sS = Sem(ctx, nc, "sS")
    sP = Sem(ctx, nc, "sP")
    dX = [Sem(ctx, nc, "dX0"), Sem(ctx, nc, "dX1")]
    dW = [Sem(ctx, nc, "dW0"), Sem(ctx, nc, "dW1")]
    dS = [Sem(ctx, nc, "dS0"), Sem(ctx, nc, "dS1")]
    dR = Sem(ctx, nc, "dR")
    dB = Sem(ctx, nc, "dB")
    dE = [Sem(ctx, nc, "dE0"), Sem(ctx, nc, "dE1")]
    dEs = [Sem(ctx, nc, "dEs0"), Sem(ctx, nc, "dEs1")]
    dC = Sem(ctx, nc, "dC")
    dO = Sem(ctx, nc, "dO")
    dY = Sem(ctx, nc, "dY")
    dG = Sem(ctx, nc, "dG")
    dRT = Sem(ctx, nc, "dRT")
    dW2 = [Sem(ctx, nc, "dW2_0"), Sem(ctx, nc, "dW2_1")]
    dC2 = Sem(ctx, nc, "dC2")
    dZ = Sem(ctx, nc, "dZ")
    dGA = [Sem(ctx, nc, "dGA0"), Sem(ctx, nc, "dGA1")]
    dGB = [Sem(ctx, nc, "dGB0"), Sem(ctx, nc, "dGB1")]

    g_ones = E(nc.sbuf_tensor("g_ones", [128, 1], BF16))
    g_onesf = E(nc.sbuf_tensor("g_onesf", [128, 1], F32))
    g_onesw = E(nc.sbuf_tensor("g_onesw", [128, 256], BF16))
    g_idbf = E(nc.sbuf_tensor("g_idbf", [128, 128], BF16))
    g_idf32 = E(nc.sbuf_tensor("g_idf32", [128, 128], F32))
    g_dg0 = E(nc.sbuf_tensor("g_dg0", [128, 256], BF16))
    g_dg1 = E(nc.sbuf_tensor("g_dg1", [128, 128], BF16))
    g_smask = E(nc.sbuf_tensor("g_smask", [128, 1], F32))
    g_convw = E(nc.sbuf_tensor("g_convw", [128, NCC, 4], F32))
    g_convb = E(nc.sbuf_tensor("g_convb", [128, NCC], F32))
    g_dtb = E(nc.sbuf_tensor("g_dtb", [NH, 1], F32))
    g_alog = E(nc.sbuf_tensor("g_alog", [NH, 1], F32))
    g_dpp = E(nc.sbuf_tensor("g_dpp", [128, 32], F32))
    g_normw = E(nc.sbuf_tensor("g_normw", [128, 32], F32))

    c_dtT = contextlib.ExitStack()
    g_dtT = c_dtT.enter_context(nc.sbuf_tensor("g_dtT", [128, 2 * NCH, 2, NH], F32))
    g_dsT = c_dtT.enter_context(nc.sbuf_tensor("g_dsT", [128, 2 * NCH, 2, NH], F32))
    diagw = c_dtT.enter_context(nc.sbuf_tensor("diagw", [128, 5, NCC, 128], BF16,
                                               side="right"))
    c_dtr = contextlib.ExitStack()
    g_dt_raw = c_dtr.enter_context(nc.sbuf_tensor("g_dt_raw", [NH, TOKT], F32))
    c_xln = contextlib.ExitStack()
    xln = c_xln.enter_context(nc.sbuf_tensor("xln", [128, KC, TOKT], BF16))

    with contextlib.ExitStack() as c0:
        e0 = c0.enter_context
        xf = [e0(nc.sbuf_tensor(f"xf{i}", [128, TOKT], F32)) for i in range(2)]
        sqb = e0(nc.sbuf_tensor("sqb", [128, TOKT], BF16))
        srow = e0(nc.sbuf_tensor("srow", [1, 3 * 512], F32))
        mB = e0(nc.sbuf_tensor("mB", [128, TOKT], BF16))
        rB = e0(nc.sbuf_tensor("rB", [128, TOKT], BF16))
        t1b = [e0(nc.sbuf_tensor(f"t1b{i}", [128, TOKT], BF16)) for i in range(2)]
        pstat = [e0(nc.psum_tensor(f"pstat{i}", [1, 512], F32)) for i in range(8)]

        with nc.Block() as B:
            def ld_consts(sync):
                for dst, src in [(g_ones, ones_col), (g_onesf, ones_col_f),
                                 (g_onesw, ones_w), (g_idbf, id_bf),
                                 (g_idf32, id_f32), (g_dg0, dg0_c), (g_dg1, dg1_c),
                                 (g_smask, din["smask"]), (g_dtb, din["dtb"]),
                                 (g_alog, din["alog"]), (g_dpp, din["dpp"]),
                                 (g_normw, din["normw"]), (g_convb, din["convb"])]:
                    dX[0].inc(sync.dma_start(out=dst[:], in_=src[:]), 16)
                dX[0].inc(sync.dma_start(
                    out=g_convw[:], in_=din["convw"][:].rearrange("p (c t) -> p c t", t=4)), 16)
            B.sync(ld_consts)


            sc_marks = []
            cp_marks = []
            pe_marks = []
            for fc in range(KC):
                sl = fc % 2

                def ld_x(sync, fc=fc, sl=sl):
                    if fc >= 2:
                        sync.wait_ge(sS.h, sc_marks[fc - 2])
                        sync.wait_ge(sV.h, cp_marks[fc - 2])
                    dX[sl].inc(sync.dma_start(
                        out=xf[sl][:], in_=din["xin"][fc * 128:(fc + 1) * 128, :]), 16)
                B.sync(ld_x)

                def do_cp(vector, fc=fc, sl=sl):
                    vector.wait_ge(dX[sl].h, dX[sl].n)
                    cp_marks.append(sV.inc(vector.tensor_copy(xln[:, fc, :], xf[sl][:])))
                B.vector(do_cp)

                def do_sq(scalar, fc=fc, sl=sl):
                    scalar.wait_ge(dX[sl].h, dX[sl].n)
                    if fc >= 1:
                        scalar.wait_ge(sT.h, pe_marks[fc - 1])
                    sc_marks.append(sS.inc(scalar.activation(sqb[:], xf[sl][:], AF.Square)))
                B.scalar(do_sq)

                def do_pe(tensor, fc=fc):
                    tensor.wait_ge(sS.h, sc_marks[fc])
                    tensor.wait_ge(sV.h, cp_marks[fc])
                    for ts in range(4):
                        tensor.matmul(pstat[ts][:, :], g_ones[:, :],
                                      xln[:, fc, ts * 512:(ts + 1) * 512],
                                      start=(fc == 0), stop=(fc == KC - 1))
                    for ts in range(4):
                        mm = tensor.matmul(pstat[4 + ts][:, :], g_ones[:, :],
                                           sqb[:, ts * 512:(ts + 1) * 512],
                                           start=(fc == 0), stop=(fc == KC - 1))
                    pe_marks.append(sT.inc(mm))
                B.tensor(do_pe)

            for ts in range(4):
                def fin_ts(vector, ts=ts):
                    if ts == 0:
                        vector.wait_ge(sT.h, pe_marks[-1])
                    if ts >= 1:
                        vector.wait_ge(dR.h, 32 * ts)
                    vector.tensor_scalar(srow[0:1, 0:512], pstat[ts][:, :], 1.0 / D, None, op0=ALU.mult)
                    m1 = sV.inc(vector.tensor_scalar(srow[0:1, 512:1024], pstat[4 + ts][:, :],
                                                     1.0 / D, None, op0=ALU.mult))
                    vector.wait_ge(sV.h, m1)
                    m2 = sV.inc(vector.tensor_mul(srow[0:1, 1024:1536], srow[0:1, 0:512], srow[0:1, 0:512]))
                    vector.wait_ge(sV.h, m2)
                    sV.inc(vector.tensor_sub(srow[0:1, 512:1024], srow[0:1, 512:1024], srow[0:1, 1024:1536]))
                B.vector(fin_ts)
                VD = sV.n

                def sq_ts(scalar, ts=ts, VD=VD):
                    scalar.wait_ge(sV.h, VD)
                    sS.inc(scalar.activation(srow[0:1, 1024:1536], srow[0:1, 512:1024],
                                             AF.Sqrt, bias=np.float32(EPS)))
                B.scalar(sq_ts)
                SD = sS.n

                def rc_ts(vector, ts=ts, SD=SD):
                    vector.wait_ge(sS.h, SD)
                    sV.inc(vector.reciprocal(srow[0:1, 512:1024], srow[0:1, 1024:1536]))
                B.vector(rc_ts)
                RD = sV.n

                def st_ts(sync, ts=ts, RD=RD):
                    sync.wait_ge(sV.h, RD)
                    dR.inc(sync.dma_start(out=mr_scr[0:1, ts * 512:(ts + 1) * 512],
                                          in_=srow[0:1, 0:512]), 16)
                    dR.inc(sync.dma_start(out=mr_scr[1:2, ts * 512:(ts + 1) * 512],
                                          in_=srow[0:1, 512:1024]), 16)
                B.sync(st_ts)

            def row_wait(sync):
                sync.wait_ge(dR.h, dR.n)
            B.sync(row_wait)

            def row_bcast(gpsimd):
                gpsimd.wait_ge(dR.h, dR.n)
                dG.inc(gpsimd.dma_start(out=mB[:], in_=mr_scr[0:1, :].to_broadcast([128, TOKT])), 16)
                dG.inc(gpsimd.dma_start(out=rB[:], in_=mr_scr[1:2, :].to_broadcast([128, TOKT])), 16)
            B.gpsimd(row_bcast)
            MRB_DONE = dG.n

            v_marks2 = []
            p_marks2 = []
            for fc in range(KC):
                sl = fc % 2

                def do_sub(vector, fc=fc, sl=sl):
                    if fc == 0:
                        vector.wait_ge(dG.h, MRB_DONE)
                    if fc >= 2:
                        vector.wait_ge(sV.h, p_marks2[fc - 2])
                    v_marks2.append(sV.inc(vector.tensor_sub(t1b[sl][:], xln[:, fc, :], mB[:])))
                B.vector(do_sub)

                def do_mul(vector, fc=fc, sl=sl):
                    vector.wait_ge(sV.h, v_marks2[fc])
                    p_marks2.append(sV.inc(vector.tensor_mul(xln[:, fc, :], t1b[sl][:], rB[:])))
                B.vector(do_mul)

            if debug:
                def dump_xln(gpsimd):
                    gpsimd.wait_ge(sV.h, p_marks2[-1])
                    for fc in range(KC):
                        dG.inc(gpsimd.dma_start(out=dbg["d_xln"][fc * 128:(fc + 1) * 128, :],
                                                in_=xln[:, fc, :]), 16)
                    gpsimd.wait_ge(dG.h, dG.n)
                B.gpsimd(dump_xln)

    with contextlib.ExitStack() as c1:
        e1 = c1.enter_context
        wblk = [e1(nc.sbuf_tensor(f"wblk{i}", [128, KC, 512], BF16)) for i in range(2)]
        stg = [e1(nc.sbuf_tensor(f"stg{i}", [128, TOKT + 4], BF16)) for i in range(2)]
        pq = [e1(nc.psum_tensor(f"pq{i}", [128, 512], F32)) for i in range(8)]

        blocks = [(DIN + 512 * i, 512) for i in range(8)]
        blocks.append((DIN + 4096, 320))
        blocks += [(512 * i, 512) for i in range(8)]

        with nc.Block() as B:
            def pad_init(vector):
                for i in range(2):
                    vector.memset(stg[i][:, 0:3], 0.0)
                    sV.inc(vector.memset(stg[i][:, 2051:2052], 0.0))
            B.vector(pad_init)

            pe_blk_marks = {}
            ot_marks = {}
            v_ot_marks = {}
            tile_idx = 0
            for bi, (of0, width) in enumerate(blocks):
                is_z = of0 < DIN
                sl = bi % 2
                ntok = TOK if is_z else TOKT
                toff = TOK if is_z else 0
                nts = ntok // 512

                def ld_w(gpsimd, bi=bi, of0=of0, width=width, sl=sl):
                    if bi >= 2:
                        gpsimd.wait_ge(sT.h, pe_blk_marks[bi - 2])
                    src = din["w1t"][:, of0:of0 + width].rearrange("(kc p) o -> p kc o", p=128)
                    dW[sl].inc(gpsimd.dma_start(out=wblk[sl][:, :, 0:width], in_=src), 16)
                    if bi == 1:
                        dsrc = din["convd"][:, :].rearrange("(t c p) m -> p t c m", p=128, t=5)
                        for tap in range(5):
                            for ch in range(2):
                                dG.inc(gpsimd.dma_start(
                                    out=diagw[:, tap, ch * 17:(ch + 1) * 17, :],
                                    in_=dsrc[:, tap, ch * 17:(ch + 1) * 17, :]), 16)
                B.gpsimd(ld_w)

                ntiles = (width + 127) // 128
                for ot in range(ntiles):
                    m = min(128, width - ot * 128)
                    of_abs = of0 + ot * 128
                    is_dt = of_abs >= 8448

                    def do_mm(tensor, bi=bi, ot=ot, m=m, sl=sl, nts=nts, toff=toff,
                              ti=tile_idx, is_last=(ot == ntiles - 1)):
                        if ot == 0:
                            tensor.wait_ge(dW[sl].h, dW[sl].n)
                        if ti >= 2:
                            tensor.wait_ge(sV.h, v_ot_marks[ti - 2])
                        pbase = 4 * (ti % 2)
                        for kc in range(KC):
                            for ts in range(nts):
                                mm = tensor.matmul(
                                    pq[pbase + ts][0:m, :],
                                    wblk[sl][:, kc, ot * 128:ot * 128 + m],
                                    xln[:, kc, toff + ts * 512:toff + (ts + 1) * 512],
                                    start=(kc == 0), stop=(kc == KC - 1))
                        ot_marks[ti] = sT.inc(mm)
                        if is_last:
                            pe_blk_marks[bi] = ot_marks[ti]
                    B.tensor(do_mm)

                    def do_evac(vector, m=m, nts=nts, is_dt=is_dt, ti=tile_idx):
                        vector.wait_ge(sT.h, ot_marks[ti])
                        ssl = ti % 2
                        pbase = 4 * (ti % 2)
                        if is_dt:
                            for ts in range(nts):
                                last = vector.tensor_copy(
                                    g_dt_raw[0:NH, ts * 512:(ts + 1) * 512],
                                    pq[pbase + ts][0:NH, :])
                        else:
                            if ti >= 2:
                                vector.wait_ge(dS[ssl].h, dS[ssl].n)
                            for ts in range(nts):
                                last = vector.tensor_copy(
                                    stg[ssl][0:m, 3 + ts * 512:3 + (ts + 1) * 512],
                                    pq[pbase + ts][0:m, :])
                        v_ot_marks[ti] = sV.inc(last)
                    B.vector(do_evac)

                    if not is_dt:
                        def do_store(sync, m=m, of_abs=of_abs, is_z=is_z, ti=tile_idx):
                            sync.wait_ge(sV.h, v_ot_marks[ti])
                            ssl = ti % 2
                            if is_z:
                                dS[ssl].inc(sync.dma_start(
                                    out=z_scr[of_abs:of_abs + m, :],
                                    in_=stg[ssl][0:m, 3:3 + TOK]), 16)
                            else:
                                dS[ssl].inc(sync.dma_start(
                                    out=xbc_scr[of_abs - DIN:of_abs - DIN + m, 0:2052],
                                    in_=stg[ssl][0:m, 0:2052]), 16)
                        B.sync(do_store)
                    tile_idx += 1

            def p1_fin(sync):
                sync.wait_ge(dS[0].h, dS[0].n)
                sync.wait_ge(dS[1].h, dS[1].n)
            B.sync(p1_fin)

    c_xln.close()

    with contextlib.ExitStack() as c2:
        e2 = c2.enter_context
        b1 = e2(nc.sbuf_tensor("b1", [NH, TOKT], F32))
        b2 = e2(nc.sbuf_tensor("b2", [NH, TOKT], F32))
        b3 = e2(nc.sbuf_tensor("b3", [NH, TOKT], F32))
        b4 = e2(nc.sbuf_tensor("b4", [NH, TOKT], F32))
        anc = e2(nc.sbuf_tensor("anc", [NH, 2], F32))
        zrow = e2(nc.sbuf_tensor("zrow", [NH, CHK], F32))
        ptr2 = e2(nc.psum_tensor("ptr2", [128, NH], F32))

        with nc.Block() as B:
            def softplus_v(vector):
                m1 = sV.inc(vector.tensor_scalar(b1[:], g_dt_raw[:], g_dtb[:, 0:1], None, op0=ALU.add))
                vector.wait_ge(sV.h, m1)
                sV.inc(vector.tensor_scalar(b3[:], b1[:], 0.0, None, op0=ALU.max))
                sV.inc(vector.memset(zrow[:], 0.0))
            B.vector(softplus_v)
            SP1 = sV.n

            def softplus_s(scalar):
                scalar.wait_ge(sV.h, SP1)
                s0 = sS.inc(scalar.activation(b2[:], b1[:], AF.Abs))
                scalar.wait_ge(sS.h, s0)
                s1 = sS.inc(scalar.activation(b2[:], b2[:], AF.Exp, scale=np.float32(-1.0)))
                scalar.wait_ge(sS.h, s1)
                sS.inc(scalar.activation(b2[:], b2[:], AF.Ln, bias=np.float32(1.0)))
                sS.inc(scalar.activation(anc[:, 0:1], g_alog[:, 0:1], AF.Exp))
            B.scalar(softplus_s)
            SP2 = sS.n

            def dt_fin(vector):
                vector.wait_ge(sS.h, SP2)
                m1 = sV.inc(vector.tensor_add(b1[:], b2[:], b3[:]))
                sV.inc(vector.tensor_scalar(anc[:, 1:2], anc[:, 0:1], -1.0, None, op0=ALU.mult))
                vector.wait_ge(sV.h, sV.n)
                sV.inc(vector.tensor_scalar(b2[:], b1[:], anc[:, 1:2], None, op0=ALU.mult))
            B.vector(dt_fin)
            ADT_DONE = sV.n

            def ead_s(scalar):
                scalar.wait_ge(sV.h, ADT_DONE)
                sS.inc(scalar.activation(b3[:], b2[:], AF.Exp))
            B.scalar(ead_s)
            EAD_DONE = sS.n

            def ead_out(sync):
                sync.wait_ge(sS.h, EAD_DONE)
                dR.inc(sync.dma_start(out=ead_scr[:, :], in_=b3[:, :]), 16)
            B.sync(ead_out)
            EAD_ST = dR.n

            def acs_v(vector):
                vector.wait_ge(sV.h, ADT_DONE)
                for c in range(2 * NCH):
                    last = vector.tensor_tensor_scan(
                        b4[:, c * CHK:(c + 1) * CHK], b2[:, c * CHK:(c + 1) * CHK],
                        zrow[:, :], 0.0, op0=ALU.add, op1=ALU.add)
                sV.inc(last)
            B.vector(acs_v)
            ACS_DONE = sV.n

            def eac_s(scalar):
                scalar.wait_ge(sV.h, ACS_DONE)
                scalar.wait_ge(dR.h, EAD_ST)
                sS.inc(scalar.activation(b2[:], b4[:], AF.Exp))
            B.scalar(eac_s)
            EAC_DONE = sS.n

            def eac_out(sync):
                sync.wait_ge(sS.h, EAC_DONE)
                dR.inc(sync.dma_start(out=eac_scr[:, :], in_=b2[:, :]), 16)
                for c in range(2 * NCH):
                    dR.inc(sync.dma_start(out=cdr_scr[c:c + 1, :],
                                          in_=b2[:, c * CHK + CHK - 1:c * CHK + CHK],
                                          allow_slow_non_contiguous=True), 16)
            B.sync(eac_out)
            EAC_ST = dR.n

            def ds_v(vector):
                vector.wait_ge(dR.h, EAD_ST)
                vector.wait_ge(sV.h, ACS_DONE)
                for c in range(2 * NCH):
                    last = vector.tensor_scalar(
                        b3[:, c * CHK:(c + 1) * CHK], b4[:, c * CHK:(c + 1) * CHK],
                        b4[:, c * CHK + CHK - 1:c * CHK + CHK], None, op0=ALU.subtract)
                sV.inc(last)
            B.vector(ds_v)
            DSV_DONE = sV.n

            def ds_s(scalar):
                scalar.wait_ge(sV.h, DSV_DONE)
                sS.inc(scalar.activation(b3[:], b3[:], AF.Exp, scale=np.float32(-1.0)))
            B.scalar(ds_s)
            DS_DONE = sS.n

            tr_marks = []
            for arr, dst, wait_val, sem in [(b1, g_dtT, ADT_DONE, sV), (b3, g_dsT, DS_DONE, sS)]:
                for c in range(2 * NCH):
                    for sh in range(2):
                        def do_tr(tensor, arr=arr, c=c, sh=sh, wait_val=wait_val, sem=sem,
                                  first=(c == 0 and sh == 0)):
                            if first:
                                tensor.wait_ge(sem.h, wait_val)
                            if tr_marks:
                                tensor.wait_ge(sV.h, tr_marks[-1])
                            sT.inc(tensor.transpose(
                                ptr2[:, :], arr[0:NH, c * CHK + sh * 128:c * CHK + (sh + 1) * 128],
                                g_idf32[0:NH, 0:NH]))
                        B.tensor(do_tr)

                        def ev_tr(vector, dst=dst, c=c, sh=sh):
                            vector.wait_ge(sT.h, sT.n)
                            tr_marks.append(sV.inc(vector.tensor_copy(dst[:, c, sh, :], ptr2[:, :])))
                        B.vector(ev_tr)

            if debug:
                def dump_dt(gpsimd):
                    gpsimd.wait_ge(sV.h, sV.n)
                    gpsimd.wait_ge(sS.h, sS.n)
                    dG.inc(gpsimd.dma_start(out=dt_scr[:, :], in_=b1[:, :]), 16)
                    gpsimd.wait_ge(dG.h, dG.n)
                B.gpsimd(dump_dt)

            def st_dtT(sync):
                sync.wait_ge(sV.h, tr_marks[-1])
                dR.inc(sync.dma_start(out=dtT_scr[:, :],
                                      in_=g_dtT[:, :, :, :].rearrange("p a b c -> p (a b c)")), 16)
                dR.inc(sync.dma_start(out=dsT_scr[:, :],
                                      in_=g_dsT[:, :, :, :].rearrange("p a b c -> p (a b c)")), 16)
            B.sync(st_dtT)

            def p2_fin(sync):
                sync.wait_ge(dR.h, dR.n)
            B.sync(p2_fin)

    c_dtr.close()

    with contextlib.ExitStack() as c3:
        e3 = c3.enter_context
        state = e3(nc.sbuf_tensor("state", [128, NH * HD], F32))
        stateb = e3(nc.sbuf_tensor("stateb", [128, NH * HD], BF16))
        xbc_c = [e3(nc.sbuf_tensor(f"xbc_c{i}", [128, NCC * 260 + 8], BF16)) for i in range(2)]
        xconv = e3(nc.sbuf_tensor("xconv", [128, NCC * 260], BF16))
        tbz = e3(nc.sbuf_tensor("tbz", [128, 4096], BF16))
        xdtT = [e3(nc.sbuf_tensor(f"xdtT{i}", [128, NH * HD], BF16)) for i in range(2)]
        BT = [e3(nc.sbuf_tensor(f"BT{i}", [128, 128], BF16)) for i in range(2)]
        CBT2 = e3(nc.sbuf_tensor("CBT2", [128, 2, 384], BF16))
        C2 = e3(nc.sbuf_tensor("C2", [128, 512], BF16))
        Dsc = [e3(nc.sbuf_tensor(f"Dsc{i}", [128, 2, 384], BF16)) for i in range(2)]
        Csc = [e3(nc.sbuf_tensor(f"Csc{i}", [128, 2, CHK], BF16)) for i in range(2)]
        eadB = [e3(nc.sbuf_tensor(f"eadB{i}", [128, 2, CHK], F32)) for i in range(2)]
        eacB = [e3(nc.sbuf_tensor(f"eacB{i}", [128, 2, CHK], F32)) for i in range(2)]
        cdex = [e3(nc.sbuf_tensor(f"cdex{i}", [128, NH], F32)) for i in range(2)]
        xdtd = [e3(nc.sbuf_tensor(f"xdtd{i}", [128, 512], BF16)) for i in range(4)]
        stu = e3(nc.sbuf_tensor("stu", [128, 512], F32))
        y_c = e3(nc.sbuf_tensor("y_c", [128, 32, CHK], BF16))
        rB1 = e3(nc.sbuf_tensor("rB1", [128, CHK], F32))
        rr = e3(nc.sbuf_tensor("rr", [1, 768], F32))
        pCv = e3(nc.psum_tensor("pCv", [128, 1024], F32))
        pY = [e3(nc.psum_tensor(f"pY{i}", [128, 256], F32)) for i in range(2)]
        pT = e3(nc.psum_tensor("pT", [128, 1024], BF16))
        pCS = e3(nc.psum_tensor("pCS", [128, 512], F32))
        pE3 = e3(nc.psum_tensor("pE3", [1, 256], F32))

        xcv = xconv[:, :].rearrange("p (c t) -> p c t", t=260)

        with nc.Block() as B:
            def init_state(vector):
                vector.memset(state[:, :], 0.0)
                vector.memset(stateb[:, :], 0.0)
                vector.memset(xbc_c[0][:, NCC * 260:], 0.0)
                sV.inc(vector.memset(xbc_c[1][:, NCC * 260:], 0.0))
            B.vector(init_state)
            ST_INIT = sV.n

            WEXP = dG.n

            cm = {}
            pending_rms = []
            conv_done = set()

            def emit_ldchunk(c):
                tok0 = 3 + c * CHK

                def ld_chunk(sync, c=c, tok0=tok0):
                    if c == 0:
                        sync.wait_ge(sV.h, ST_INIT)
                    elif c >= 2:
                        sync.wait_ge(sT.h, cm[("cvmmq", c - 2, 16)])
                    if c >= 2:
                        sync.wait_ge(sV.h, cm[("supd", c - 2)])
                    src = xbc_scr[:, :].rearrange("(cc p) t -> p cc t", p=128)
                    dCp = dC if c % 2 == 0 else dC2
                    dCp.inc(sync.dma_start(
                        out=xbc_c[c % 2][:, 0:NCC * 260].rearrange("p (c t) -> p c t", t=260),
                        in_=src[:, :, tok0 - 3:tok0 + 257]), 16)
                    dCp.inc(sync.dma_start(out=cdex[c % 2][:, :],
                                          in_=cdr_scr[c:c + 1, :].to_broadcast([128, NH])), 16)
                B.sync(ld_chunk)
                cm[("chld", c)] = (dC if c % 2 == 0 else dC2).n

            def emit_conv(c, qlo, qhi):
                prev_main = c - 1 >= NCH
                for q in range(qlo, qhi):
                    def conv_mm(tensor, c=c, q=q):
                        if q == 0:
                            tensor.wait_ge((dC if c % 2 == 0 else dC2).h, cm[("chld", c)])
                            if c == 0:
                                tensor.wait_ge(dG.h, WEXP)
                        if q >= 2:
                            tensor.wait_ge(sS.h, cm[("cvev", c, q - 2)])
                        elif c >= 1:
                            tensor.wait_ge(sS.h, cm[("cvev", c - 1, 15 + q)])
                            tensor.wait_ge(sV.h, cm[("cb_ev", c - 1)])
                        for ci in range(2):
                            cc = 2 * q + ci
                            po = pCv[:, (q % 2) * 512 + ci * 256:(q % 2) * 512 + (ci + 1) * 256]
                            for tap in range(4):
                                tensor.matmul(
                                    po, diagw[:, tap, cc, :],
                                    xbc_c[c % 2][:, cc * 260 + tap:cc * 260 + tap + 256],
                                    start=(tap == 0), stop=False)
                            mm = tensor.matmul(po, diagw[:, 4, cc, :], g_onesw[:, :],
                                               start=False, stop=True)
                        cm[("cvmm", c, 2 * q + 1)] = cm[("cvmmq", c, q)] = sT.inc(mm)
                    B.tensor(conv_mm)

                    def conv_ev(scalar, c=c, q=q, prev_main=prev_main):
                        scalar.wait_ge(sT.h, cm[("cvmmq", c, q)])
                        if q == 0 and c >= 1:
                            scalar.wait_ge(sT.h, cm[("pe_cb", c - 1)])
                            if prev_main:
                                scalar.wait_ge(sV.h, cm[("yev", c - 1, 31)])
                        cm[("cvev", c, q)] = sS.inc(scalar.activation(
                            xcv[:, 2 * q:2 * q + 2, 0:256],
                            pCv[:, (q % 2) * 512:(q % 2) * 512 + 512]
                            .rearrange("p (a b) -> p a b", a=2), AF.Silu))
                    B.scalar(conv_ev)
                if qhi == 17:
                    cm[("silu", c)] = cm[("cvev", c, 16)]
                    cm[("cvmm", c, NCC - 1)] = cm[("cvmmq", c, 16)]

            for c in range(2 * NCH):
                main = c >= NCH
                prev_main = c - 1 >= NCH
                tok0 = 3 + c * CHK
                mc = c - NCH

                if c not in conv_done:
                    emit_ldchunk(c)
                    emit_conv(c, 0, 17)
                    conv_done.add(c)

                if debug and c == NCH:
                    def dump_xconv(gpsimd, c=c):
                        gpsimd.wait_ge(sS.h, cm[("silu", c)])
                        for g in range(NCC):
                            dG.inc(gpsimd.dma_start(out=dbg["d_xconv"][g * 128:(g + 1) * 128, :],
                                                    in_=xcv[:, g, 0:CHK]), 16)
                        gpsimd.wait_ge(dG.h, dG.n)
                    B.gpsimd(dump_xconv)

                def tr_bt(tensor, c=c):
                    tensor.wait_ge(sS.h, cm[("silu", c)])
                    if c >= 1:
                        tensor.wait_ge(sV.h, cm[("xdt_ev", c - 1, 1, 3)])
                    tensor.transpose(pT[:, 0:128], xcv[:, 32, 0:128], g_idbf[:, :])
                    cm[("pe_bt", c)] = sT.inc(tensor.transpose(
                        pT[:, 128:256], xcv[:, 32, 128:256], g_idbf[:, :]))
                B.tensor(tr_bt)

                def ev_bt(vector, c=c):
                    vector.wait_ge(sT.h, cm[("pe_bt", c)])
                    vector.tensor_copy(BT[0][:, :], pT[:, 0:128])
                    cm[("bt_ev", c)] = sV.inc(vector.tensor_copy(BT[1][:, :], pT[:, 128:256]))
                B.vector(ev_bt)

                for sh in range(2):
                    for g in range(4):
                        def tr_xdt(tensor, c=c, sh=sh, g=g):
                            if (sh, g) == (0, 0):
                                tensor.wait_ge(sV.h, cm[("bt_ev", c)])
                            else:
                                prev = (sh, g - 1) if g > 0 else (sh - 1, 3)
                                tensor.wait_ge(sV.h, cm[("xdt_ev", c, prev[0], prev[1])])
                            for f8 in range(8):
                                fc = g * 8 + f8
                                mm = tensor.transpose(pT[:, f8 * 128:(f8 + 1) * 128],
                                                      xcv[:, fc, sh * 128:sh * 128 + 128],
                                                      g_idbf[:, :])
                            cm[("pe_xdt", c, sh, g)] = sT.inc(mm)
                        B.tensor(tr_xdt)

                        def ev_xdt(vector, c=c, sh=sh, g=g, prev_main=prev_main):
                            vector.wait_ge(sT.h, cm[("pe_xdt", c, sh, g)])
                            if (sh, g) == (0, 0):
                                if c >= 1:
                                    vector.wait_ge(sP.h, cm[("xdtd", c - 1, 7)])
                            dtexp = _bc(g_dtT[:, c, sh, g * 16:(g + 1) * 16], 2, HD)
                            cm[("xdt_ev", c, sh, g)] = sV.inc(vector.tensor_mul(
                                xdtT[sh][:, g * 1024:(g + 1) * 1024].rearrange("p (h q) -> p h q", q=HD),
                                pT[:, 0:1024].rearrange("p (h q) -> p h q", q=HD),
                                dtexp))
                        B.vector(ev_xdt)

                def cb_mm(tensor, c=c):
                    if c >= 1:
                        tensor.wait_ge(sV.h, cm[("cb_ev", c - 1)])
                    tensor.matmul(pCv[:, 0:256], xcv[:, 32, 0:128], xcv[:, 33, 0:256],
                                  start=True, stop=True)
                    cm[("pe_cb", c)] = sT.inc(tensor.matmul(
                        pCv[:, 512:768], xcv[:, 32, 128:256], xcv[:, 33, 0:256],
                        start=True, stop=True))
                B.tensor(cb_mm)

                def cb_ev(vector, c=c):
                    vector.wait_ge(sT.h, cm[("pe_cb", c)])
                    if main and c > NCH:
                        vector.wait_ge(sP.h, cm[("m", c - 1, 31)])
                    vector.tensor_copy(CBT2[:, 0, 0:256], pCv[:, 0:256])
                    vector.tensor_copy(CBT2[:, 1, 0:256], pCv[:, 0:256])
                    vector.tensor_copy(CBT2[:, 0, 256:384], pCv[:, 640:768])
                    vector.tensor_copy(CBT2[:, 1, 256:384], pCv[:, 640:768])
                    vector.tensor_copy(C2[:, 0:256], xcv[:, 33, 0:256])
                    cm[("cb_ev", c)] = sV.inc(vector.tensor_copy(C2[:, 256:512], xcv[:, 33, 0:256]))
                B.vector(cb_ev)

                if c < NCH:
                    emit_ldchunk(c + 1)

                if pending_rms:
                    pending_rms.pop()()

                if main:
                    def emit_ld(j, c=c):
                        sl2 = j % 2

                        def ld_ead(sync, c=c, j=j, sl2=sl2):
                            if j >= 2:
                                sync.wait_ge(sV.h, cm[("scan", c, j - 2)])
                            elif c > NCH:
                                sync.wait_ge(sV.h, cm[("scan", c - 1, 30 + j)])
                            dE[sl2].inc(sync.dma_start(
                                out=eadB[sl2][:, :, :],
                                in_=ead_scr[2 * j:2 * j + 2, c * CHK:(c + 1) * CHK]
                                .unsqueeze(0).to_broadcast([128, 2, CHK])), 16)
                        B.sync(ld_ead)
                        cm[("eadld", c, j)] = dE[sl2].n

                        def ld_eac(sync, c=c, j=j, sl2=sl2):
                            if j >= 2:
                                sync.wait_ge(sP.h, cm[("m", c, j - 2)])
                            elif c > NCH:
                                sync.wait_ge(sP.h, cm[("m", c - 1, 30 + j)])
                            dEs[sl2].inc(sync.dma_start(
                                out=eacB[sl2][:, :, :],
                                in_=eac_scr[2 * j:2 * j + 2, c * CHK:(c + 1) * CHK]
                                .unsqueeze(0).to_broadcast([128, 2, CHK])), 16)
                        B.sync(ld_eac)
                        cm[("eacld", c, j)] = dEs[sl2].n

                    def emit_scan(j, c=c):
                        sl2 = j % 2

                        def do_scan(vector, c=c, j=j, sl2=sl2):
                            vector.wait_ge(dE[sl2].h, cm[("eadld", c, j)])
                            if j >= 2:
                                vector.wait_ge(sT.h, cm[("ymm", c, j - 2)])
                            elif c > NCH:
                                vector.wait_ge(sT.h, cm[("ymm", c - 1, 30 + j)])
                            for h2 in range(2):
                                vector.tensor_tensor_scan(
                                    Dsc[sl2][:, h2, 0:256], eadB[sl2][:, h2, :],
                                    g_dg0[:, :], 0.0, op0=ALU.mult, op1=ALU.max)
                                last = vector.tensor_tensor_scan(
                                    Dsc[sl2][:, h2, 256:384], eadB[sl2][:, h2, 128:256],
                                    g_dg1[:, :], 0.0, op0=ALU.mult, op1=ALU.max)
                            cm[("scan", c, j)] = sV.inc(last)
                        B.vector(do_scan)

                    def emit_m(j, c=c):
                        sl2 = j % 2

                        def do_m(pool, c=c, j=j, sl2=sl2):
                            pool.wait_ge(sV.h, cm[("scan", c, j)])
                            pool.wait_ge(dEs[sl2].h, cm[("eacld", c, j)])
                            if j <= 1:
                                pool.wait_ge(sV.h, cm[("cb_ev", c)])
                            if j >= 2:
                                pool.wait_ge(sT.h, cm[("ymm", c, j - 2)])
                            elif c > NCH:
                                pool.wait_ge(sT.h, cm[("ymm", c - 1, 30 + j)])
                            pool.tensor_mul(Dsc[sl2][:, :, :].rearrange("p a b -> p (a b)"),
                                            Dsc[sl2][:, :, :].rearrange("p a b -> p (a b)"),
                                            CBT2[:, :, :].rearrange("p a b -> p (a b)"))
                            cm[("m", c, j)] = sP.inc(pool.tensor_mul(
                                Csc[sl2][:, :, :].rearrange("p a b -> p (a b)"),
                                eacB[sl2][:, :, :].rearrange("p a b -> p (a b)"),
                                C2[:, :]))
                        B.gpsimd(do_m)

                    def emit_ymm(j, c=c):
                        sl2 = j % 2

                        def do_ymm(tensor, c=c, j=j, sl2=sl2):
                            tensor.wait_ge(sP.h, cm[("m", c, j)])
                            if j == 0:
                                tensor.wait_ge(sV.h, cm[("stb", c - 1)])
                            if j >= 2:
                                tensor.wait_ge(sV.h, cm[("yev", c, j - 2)])
                            elif c > NCH:
                                tensor.wait_ge(sV.h, cm[("yev", c - 1, 30 + j)])
                            for h2 in range(2):
                                h = 2 * j + h2
                                tp = (0, 64 * h2)
                                po = pY[sl2][h2 * 64:(h2 + 1) * 64, :]
                                tensor.matmul(po[:, 0:256],
                                              xdtT[0][:, h * HD:(h + 1) * HD],
                                              Dsc[sl2][:, h2, 0:256],
                                              start=True, stop=False, tile_position=tp)
                                tensor.matmul(po[:, 128:256],
                                              xdtT[1][:, h * HD:(h + 1) * HD],
                                              Dsc[sl2][:, h2, 256:384],
                                              start=False, stop=False, tile_position=tp)
                                mm = tensor.matmul(po[:, 0:256],
                                                   stateb[:, h * HD:(h + 1) * HD],
                                                   Csc[sl2][:, h2, :],
                                                   start=False, stop=True, tile_position=tp)
                            cm[("ymm", c, j)] = sT.inc(mm)
                        B.tensor(do_ymm)

                    def emit_yev(j, c=c):
                        sl2 = j % 2

                        def do_yev(vector, c=c, j=j, sl2=sl2):
                            vector.wait_ge(sT.h, cm[("ymm", c, j)])
                            if j == 0 and c > NCH:
                                vector.wait_ge(dY.h, cm[("yndma", c - 1)])
                            cm[("yev", c, j)] = sV.inc(vector.scalar_tensor_tensor(
                                y_c[:, j, :], xcv[:, j, 0:CHK], g_dpp[:, j:j + 1],
                                pY[sl2][:, :], op0=ALU.mult, op1=ALU.add))
                        B.vector(do_yev)

                    for st in range(34):
                        if st < 32:
                            emit_ld(st)
                        if 0 <= st - 1 < 32:
                            emit_m(st - 1)
                        if 0 <= st - 2 < 32:
                            emit_ymm(st - 2)
                        if st < 32:
                            emit_scan(st)
                        if 0 <= st - 2 < 32:
                            emit_yev(st - 2)

                for hg in range(8):
                    def do_xdtd(pool, c=c, hg=hg):
                        if hg == 0:
                            pool.wait_ge(sV.h, cm[("xdt_ev", c, 1, 3)])
                        elif hg >= 2:
                            pool.wait_ge(sT.h, cm[("cs_mm", c, hg - 2)])
                        par = 2 * (hg % 2)
                        for sh in range(2):
                            dsexp = _bc(g_dsT[:, c, sh, hg * 8:(hg + 1) * 8], 2, HD)
                            last = pool.tensor_mul(
                                xdtd[par + sh][:, :].rearrange("p (h q) -> p h q", q=HD),
                                xdtT[sh][:, hg * 512:(hg + 1) * 512].rearrange("p (h q) -> p h q", q=HD),
                                dsexp)
                        cm[("xdtd", c, hg)] = sP.inc(last)
                    B.gpsimd(do_xdtd)

                    def do_csmm(tensor, c=c, hg=hg, main=main):
                        tensor.wait_ge(sP.h, cm[("xdtd", c, hg)])
                        if hg == 0:
                            tensor.wait_ge(sV.h, cm[("bt_ev", c)])
                        if hg >= 1:
                            tensor.wait_ge(sV.h, cm[("supd_h", c, hg - 1)])
                        par = 2 * (hg % 2)
                        tensor.matmul(pCS[:, :], BT[0][:, :], xdtd[par][:, :],
                                      start=True, stop=False)
                        cm[("cs_mm", c, hg)] = sT.inc(tensor.matmul(
                            pCS[:, :], BT[1][:, :], xdtd[par + 1][:, :],
                            start=False, stop=True))
                    B.tensor(do_csmm)

                    def do_supd(vector, c=c, hg=hg):
                        vector.wait_ge(sT.h, cm[("cs_mm", c, hg)])
                        m1 = sV.inc(vector.tensor_mul(
                            stu[:, :].rearrange("p (h q) -> p h q", q=HD),
                            state[:, hg * 512:(hg + 1) * 512].rearrange("p (h q) -> p h q", q=HD),
                            _bc(cdex[c % 2][:, hg * 8:(hg + 1) * 8], 2, HD)))
                        vector.wait_ge(sV.h, m1)
                        cm[("supd_h", c, hg)] = sV.inc(vector.tensor_add(
                            state[:, hg * 512:(hg + 1) * 512], stu[:, :], pCS[:, :]))
                    B.vector(do_supd)

                    if c < NCH:
                        emit_conv(c + 1, 2 * hg, 2 * hg + 2)
                cm[("supd", c)] = cm[("supd_h", c, 7)]
                if c < NCH:
                    emit_conv(c + 1, 16, 17)
                    conv_done.add(c + 1)

                def post_state(vector, c=c, main=main):
                    vector.wait_ge(sV.h, cm[("supd", c)])
                    if c == NCH - 1:
                        m1 = sV.inc(vector.tensor_scalar(
                            state[:, :], state[:, :], g_smask[:, 0:1], None, op0=ALU.mult))
                        vector.wait_ge(sV.h, m1)
                    if main:
                        vector.wait_ge(sT.h, cm[("ymm", c, 31)])
                    cm[("stb", c)] = sV.inc(vector.tensor_copy(stateb[:, :], state[:, :]))
                B.vector(post_state)

                if debug and c == NCH - 1:
                    def dump_state(gpsimd, c=c):
                        gpsimd.wait_ge(sV.h, cm[("stb", c)])
                        dG.inc(gpsimd.dma_start(out=dbg["d_state"][:, :], in_=state[:, :]), 16)
                        gpsimd.wait_ge(dG.h, dG.n)
                    B.gpsimd(dump_state)

                def emit_rms(c=c, mc=mc):
                    if debug:
                        def dump_y(gpsimd, c=c, mc=mc):
                            gpsimd.wait_ge(sV.h, cm[("yev", c, 31)])
                            for fc in range(32):
                                dG.inc(gpsimd.dma_start(
                                    out=dbg["d_y"][fc * 128:(fc + 1) * 128, mc * CHK:(mc + 1) * CHK],
                                    in_=y_c[:, fc, :]), 16)
                            gpsimd.wait_ge(dG.h, dG.n)
                            cm[("dumpy", c)] = dG.n
                        B.gpsimd(dump_y)

                    for hf in range(2):
                        z_h = tbz[:, 0:4096].rearrange("p (a b) -> p a b", a=16)
                        z_hf = tbz[:, 0:4096]

                        def ld_z(sync, c=c, hf=hf, mc=mc, z_h=z_h):
                            if hf == 0:
                                if c > NCH:
                                    sync.wait_ge(sP.h, cm[("yn", c - 1, 1)])
                            else:
                                sync.wait_ge(sT.h, cm[("sqmm", c, 0)])
                                sync.wait_ge(sV.h, cm[("y32", c, 0)])
                            zsrc = z_scr[:, :].rearrange("(fc p) t -> p fc t", p=128)
                            dZ.inc(sync.dma_start(
                                out=z_h,
                                in_=zsrc[:, hf * 16:(hf + 1) * 16, mc * CHK:(mc + 1) * CHK]), 16)
                        B.sync(ld_z)
                        ZLD = dZ.n

                        def z_silu(scalar, c=c, hf=hf, ZLD=ZLD, z_hf=z_hf):
                            scalar.wait_ge(dZ.h, ZLD)
                            cm[("zsilu", c, hf)] = sS.inc(scalar.activation(
                                z_hf, z_hf, AF.Silu))
                        B.scalar(z_silu)

                        def y32_mul(vector, c=c, hf=hf, z_hf=z_hf):
                            vector.wait_ge(sS.h, cm[("zsilu", c, hf)])
                            vector.wait_ge(sV.h, cm[("yev", c, 15 + 16 * hf)])
                            if ("dumpy", c) in cm:
                                vector.wait_ge(dG.h, cm[("dumpy", c)])
                            cm[("y32", c, hf)] = sV.inc(vector.tensor_mul(
                                y_c[:, hf * 16:(hf + 1) * 16, :].rearrange("p a b -> p (a b)"),
                                y_c[:, hf * 16:(hf + 1) * 16, :].rearrange("p a b -> p (a b)"),
                                z_hf))
                        B.vector(y32_mul)

                        def sq_mul(vector, c=c, hf=hf, z_hf=z_hf):
                            vector.wait_ge(sV.h, cm[("y32", c, hf)])
                            yflat = y_c[:, hf * 16:(hf + 1) * 16, :].rearrange(
                                "p a b -> p (a b)")
                            cm[("sq", c, hf)] = sV.inc(vector.tensor_mul(
                                z_hf, yflat, yflat))
                        B.vector(sq_mul)

                        def sq_mm(tensor, c=c, hf=hf, z_h=z_h):
                            tensor.wait_ge(sV.h, cm[("sq", c, hf)])
                            if c > NCH and hf == 0:
                                tensor.wait_ge(sV.h, cm[("rsq", c - 1)])
                            for f16 in range(16):
                                mm = tensor.matmul(pE3[:, :], g_ones[:, :],
                                                   z_h[:, f16, :],
                                                   start=(hf == 0 and f16 == 0),
                                                   stop=(hf == 1 and f16 == 15))
                            cm[("sqmm", c, hf)] = sT.inc(mm)
                        B.tensor(sq_mm)

                    def r_rsqrt(vector, c=c):
                        vector.wait_ge(sT.h, cm[("sqmm", c, 1)])
                        if c > NCH:
                            vector.wait_ge(dB.h, cm[("rb1", c - 1)])
                        v = rr[0:1, 0:256]
                        y = rr[0:1, 256:512]
                        t = rr[0:1, 512:768]
                        I32 = mybir.dt.int32
                        m = sV.inc(vector.tensor_scalar(v, pE3[0:1, :], 1.0 / DIN,
                                                        float(EPS), op0=ALU.mult, op1=ALU.add))
                        vector.wait_ge(sV.h, m)
                        m = sV.inc(vector.tensor_scalar(t.bitcast(I32), v.bitcast(I32), 1, None,
                                                        op0=ALU.arith_shift_right))
                        vector.wait_ge(sV.h, m)
                        m = sV.inc(vector.tensor_scalar(y.bitcast(I32), t.bitcast(I32), -1, 0x5f3759df,
                                                        op0=ALU.mult, op1=ALU.add))
                        for _ in range(2):
                            vector.wait_ge(sV.h, m)
                            m = sV.inc(vector.tensor_mul(t, y, y))
                            vector.wait_ge(sV.h, m)
                            m = sV.inc(vector.tensor_mul(t, t, v))
                            vector.wait_ge(sV.h, m)
                            m = sV.inc(vector.tensor_scalar(t, t, -0.5, 1.5,
                                                            op0=ALU.mult, op1=ALU.add))
                            vector.wait_ge(sV.h, m)
                            m = sV.inc(vector.tensor_mul(y, y, t))
                        cm[("rsq", c)] = m
                    B.vector(r_rsqrt)

                    def r_bounce(sync, c=c):
                        sync.wait_ge(sV.h, cm[("rsq", c)])
                        dR.inc(sync.dma_start(out=rr_scr[0:1, :], in_=rr[0:1, 256:512]), 16)
                        sync.wait_ge(dR.h, dR.n)
                        dB.inc(sync.dma_start(out=rB1[:, :],
                                              in_=rr_scr[0:1, :].to_broadcast([128, CHK])), 16)
                        cm[("rb1", c)] = dB.n
                    B.sync(r_bounce)

                    for hf in range(2):
                        ynt = tbz[:, 0:4096]

                        def yn_mul(pool, c=c, hf=hf, mc=mc, ynt=ynt):
                            pool.wait_ge(dB.h, cm[("rb1", c)])
                            pool.wait_ge(sT.h, cm[("sqmm", c, 1)])
                            if hf == 1:
                                pool.wait_ge(sP.h, cm[("yn", c, 0)])
                            wexp2 = _bc(g_normw[:, hf * 16:(hf + 1) * 16], 2, CHK)
                            m1 = sP.inc(pool.tensor_mul(
                                ynt.rearrange("p (a b) -> p a b", a=16),
                                y_c[:, hf * 16:(hf + 1) * 16, :], wexp2))
                            pool.wait_ge(sP.h, m1)
                            cm[("yn", c, hf)] = sP.inc(pool.tensor_mul(
                                y_c[:, hf * 16:(hf + 1) * 16, :],
                                ynt.rearrange("p (a b) -> p a b", a=16),
                                _bc(rB1[:, :], 1, 16)))
                        B.gpsimd(yn_mul)

                        def yn_st(sync, c=c, hf=hf, mc=mc):
                            sync.wait_ge(sP.h, cm[("yn", c, hf)])
                            dst = yn_scr[:, :].rearrange("(fc p) t -> p fc t", p=128)
                            dY.inc(sync.dma_start(
                                out=dst[:, hf * 16:(hf + 1) * 16, mc * CHK:(mc + 1) * CHK],
                                in_=y_c[:, hf * 16:(hf + 1) * 16, :]), 16)
                            cm[("yndma", c)] = dY.n
                        B.sync(yn_st)

                if main:
                    pending_rms.append(emit_rms)

            while pending_rms:
                pending_rms.pop()()

            def p3_fin(sync):
                sync.wait_ge(dY.h, dY.n)
                sync.wait_ge(sV.h, cm[("stb", 2 * NCH - 1)])
            B.sync(p3_fin)

    c_dtT.close()

    c45 = contextlib.ExitStack()
    g_xln2 = c45.enter_context(nc.sbuf_tensor("g_xln2", [128, KC, TOK], BF16, side="right"))

    with contextlib.ExitStack() as c4:
        e4 = c4.enter_context
        ynb = e4(nc.sbuf_tensor("ynb", [128, 32, TOK], BF16))
        w2og = [e4(nc.sbuf_tensor(f"w2og{i}", [128, 32, 512], BF16)) for i in range(2)]
        xres = [e4(nc.sbuf_tensor(f"xres{i}", [128, TOK], F32)) for i in range(4)]
        x2st = [e4(nc.sbuf_tensor(f"x2st{i}", [128, TOK], F32)) for i in range(4)]
        sqst = [e4(nc.sbuf_tensor("sqst0", [128, TOK], F32))]
        srow4 = e4(nc.sbuf_tensor("srow4", [1, 3 * 512], F32))

        with nc.Block() as B:
            p4m = {}
            with contextlib.ExitStack() as cA:
                pq4 = [cA.enter_context(nc.psum_tensor(f"pq4{i}", [128, 512], F32)) for i in range(4)]
                pst4 = [cA.enter_context(nc.psum_tensor(f"pst4{i}", [1, 512], F32)) for i in range(4)]

                ynld = []
                for kc in range(32):
                    def ld_yn(sync, kc=kc):
                        dX[0].inc(sync.dma_start(
                            out=ynb[:, kc, :],
                            in_=yn_scr[kc * 128:(kc + 1) * 128, :]), 16)
                        ynld.append(dX[0].n)
                    B.sync(ld_yn)

                for og in range(4):
                    sl = og % 2

                    def ld_w2(gpsimd, og=og, sl=sl):
                        if og >= 2:
                            gpsimd.wait_ge(sT.h, p4m[("mm", og - 2, 1)])
                        src = din["w2t"][:, og * 512:(og + 1) * 512].rearrange(
                            "(kc p) o -> p kc o", p=128)
                        dW[sl].inc(gpsimd.dma_start(out=w2og[sl][:, :, :], in_=src), 16)
                        p4m[("w2ld", og)] = dW[sl].n
                    B.gpsimd(ld_w2)

                    for ts in range(2):
                        def do_mm4(tensor, og=og, ts=ts, sl=sl):
                            tensor.wait_ge(dW[sl].h, p4m[("w2ld", og)])
                            if og == 0 and ts == 0:
                                tensor.wait_ge(dX[0].h, ynld[-1])
                            else:
                                pog, pts = (og, 0) if ts == 1 else (og - 1, 1)
                                for oi in range(4):
                                    tensor.wait_ge(sV.h, p4m[("x2", pog, oi, pts)])
                            for kc in range(32):
                                for oi in range(4):
                                    mm = tensor.matmul(
                                        pq4[oi][:, :],
                                        w2og[sl][:, kc, oi * 128:(oi + 1) * 128],
                                        ynb[:, kc, ts * 512:(ts + 1) * 512],
                                        start=(kc == 0), stop=(kc == 31))
                            p4m[("mm", og, ts)] = sT.inc(mm)
                        B.tensor(do_mm4)

                        if ts == 0:
                            for oi in range(4):
                                def ld_xres(sync, oi=oi, og=og, ot=og * 4 + oi):
                                    if og > 0:
                                        sync.wait_ge(sV.h, p4m[("x2", og - 1, oi, 1)])
                                    dC.inc(sync.dma_start(
                                        out=xres[oi][:, :],
                                        in_=din["xin"][ot * 128:(ot + 1) * 128, TOK:TOKT]), 16)
                                B.sync(ld_xres)
                            p4m[("xrl", og)] = dC.n

                        for oi in range(4):
                            ot = og * 4 + oi
                            xsl = oi % 2

                            def do_x2(vector, og=og, oi=oi, ts=ts, ot=ot, xsl=xsl):
                                vector.wait_ge(sT.h, p4m[("mm", og, ts)])
                                vector.wait_ge(dC.h, p4m[("xrl", og)])
                                if ts == 0 and ot >= 4:
                                    pot = ot - 4
                                    vector.wait_ge(dS[pot % 2].h, p4m[("x2dma", pot)])
                                    vector.wait_ge(sT.h, p4m[("stat", pot)])
                                p4m[("x2", og, oi, ts)] = sV.inc(vector.tensor_add(
                                    x2st[oi][:, ts * 512:(ts + 1) * 512], pq4[oi][:, :],
                                    xres[oi][:, ts * 512:(ts + 1) * 512]))
                            B.vector(do_x2)

                            if ts == 1:
                                def do_sq4(vector, og=og, oi=oi, ot=ot, xsl=xsl):
                                    vector.wait_ge(sV.h, p4m[("x2", og, oi, 1)])
                                    if ot >= 1:
                                        vector.wait_ge(sT.h, p4m[("stat", ot - 1)])
                                    p4m[("sq4", ot)] = sV.inc(vector.tensor_mul(
                                        sqst[0][:, :], x2st[oi][:, :], x2st[oi][:, :]))
                                B.vector(do_sq4)

                                def do_stat4(tensor, oi=oi, ot=ot, xsl=xsl):
                                    tensor.wait_ge(sV.h, p4m[("sq4", ot)])
                                    for ts2 in range(2):
                                        tensor.matmul(pst4[ts2][:, :], g_onesf[:, :],
                                                      x2st[oi][:, ts2 * 512:(ts2 + 1) * 512],
                                                      start=(ot == 0), stop=(ot == 15))
                                    for ts2 in range(2):
                                        mm = tensor.matmul(pst4[2 + ts2][:, :], g_onesf[:, :],
                                                           sqst[0][:, ts2 * 512:(ts2 + 1) * 512],
                                                           start=(ot == 0), stop=(ot == 15))
                                    p4m[("stat", ot)] = sT.inc(mm)
                                B.tensor(do_stat4)

                                def st_x2(sync, oi=oi, ot=ot):
                                    sync.wait_ge(sV.h, p4m[("sq4", ot)])
                                    if ot >= 2:
                                        sync.wait_ge(dS[ot % 2].h, p4m[("x2dma", ot - 2)])
                                    dS[ot % 2].inc(sync.dma_start(
                                        out=x2_scr[ot * 128:(ot + 1) * 128, :],
                                        in_=x2st[oi][:, :]), 16)
                                    p4m[("x2dma", ot)] = dS[ot % 2].n
                                B.sync(st_x2)

                for ts2 in range(2):
                    def fin4(vector, ts2=ts2):
                        if ts2 == 0:
                            vector.wait_ge(sT.h, p4m[("stat", 15)])
                        else:
                            vector.wait_ge(dR.h, dR.n)
                        vector.tensor_scalar(srow4[0:1, 0:512], pst4[ts2][:, :], 1.0 / D, None, op0=ALU.mult)
                        m1 = sV.inc(vector.tensor_scalar(srow4[0:1, 512:1024], pst4[2 + ts2][:, :],
                                                         1.0 / D, None, op0=ALU.mult))
                        vector.wait_ge(sV.h, m1)
                        m2 = sV.inc(vector.tensor_mul(srow4[0:1, 1024:1536], srow4[0:1, 0:512],
                                                      srow4[0:1, 0:512]))
                        vector.wait_ge(sV.h, m2)
                        sV.inc(vector.tensor_sub(srow4[0:1, 512:1024], srow4[0:1, 512:1024],
                                                 srow4[0:1, 1024:1536]))
                    B.vector(fin4)
                    VD4 = sV.n

                    def sq4(scalar, ts2=ts2, VD4=VD4):
                        scalar.wait_ge(sV.h, VD4)
                        sS.inc(scalar.activation(srow4[0:1, 1024:1536], srow4[0:1, 512:1024],
                                                 AF.Sqrt, bias=np.float32(EPS)))
                    B.scalar(sq4)
                    SD4 = sS.n

                    def rc4(vector, ts2=ts2, SD4=SD4):
                        vector.wait_ge(sS.h, SD4)
                        sV.inc(vector.reciprocal(srow4[0:1, 512:1024], srow4[0:1, 1024:1536]))
                    B.vector(rc4)
                    RD4 = sV.n

                    def st4(sync, ts2=ts2, RD4=RD4):
                        sync.wait_ge(sV.h, RD4)
                        dR.inc(sync.dma_start(out=r2_scr[0:1, ts2 * 512:(ts2 + 1) * 512],
                                              in_=srow4[0:1, 0:512]), 16)
                        dR.inc(sync.dma_start(out=r2_scr[1:2, ts2 * 512:(ts2 + 1) * 512],
                                              in_=srow4[0:1, 512:1024]), 16)
                        sync.wait_ge(dR.h, dR.n)
                    B.sync(st4)

                def p4_fin(sync):
                    sync.wait_ge(dS[0].h, dS[0].n)
                    sync.wait_ge(dS[1].h, dS[1].n)
                    sync.wait_ge(dR.h, dR.n)
                B.sync(p4_fin)

    with contextlib.ExitStack() as c5:
        e5 = c5.enter_context
        acc = e5(nc.sbuf_tensor("acc", [128, 16, TOK], F32))
        hbuf = [e5(nc.sbuf_tensor(f"hbuf{i}", [128, 6, TOK], BF16)) for i in range(2)]
        wgu = [e5(nc.sbuf_tensor(f"wgu{i}", [128, 16, 512], BF16)) for i in range(2)]
        cB5 = [e5(nc.sbuf_tensor(f"cB5{i}", [128, TOK], F32)) for i in range(2)]
        sgb = e5(nc.sbuf_tensor("sgb", [128, TOK], BF16))
        tgu = e5(nc.sbuf_tensor("tgu", [128, TOK], BF16))
        wrtb = e5(nc.sbuf_tensor("wrtb", [128, KC, NE], BF16))
        scores = e5(nc.sbuf_tensor("scores", [16, TOK], F32))
        scT = e5(nc.sbuf_tensor("scT", [128, 8, NE], F32))
        work = e5(nc.sbuf_tensor("work", [128, 8, NE], F32))
        selacc = e5(nc.sbuf_tensor("selacc", [128, 8, NE], F32))
        eqb = e5(nc.sbuf_tensor("eqb", [128, 8, NE], F32))
        redm = e5(nc.sbuf_tensor("redm", [128, 8, 2], F32))
        ebB = e5(nc.sbuf_tensor("ebB", [128, NE], F32))
        cmb16 = e5(nc.sbuf_tensor("cmb16", [16, TOK], F32))
        obuf = [e5(nc.sbuf_tensor(f"obuf{i}", [128, TOK], F32)) for i in range(2)]
        xr5 = [e5(nc.sbuf_tensor(f"xr5{i}", [128, TOK], F32)) for i in range(2)]
        x2l = xr5
        mB2 = e5(nc.sbuf_tensor("mB2", [128, TOK], F32))
        rB2 = e5(nc.sbuf_tensor("rB2", [128, TOK], F32))
        t2b = e5(nc.sbuf_tensor("t2b", [128, TOK], BF16))

        p5m = {}
        with nc.Block() as B:
            with contextlib.ExitStack() as cr:
                pr = cr.enter_context(nc.psum_tensor("pr", [16, TOK], F32))
                ptrA = cr.enter_context(nc.psum_tensor("ptrA", [128, NE], F32))
                ptrB = cr.enter_context(nc.psum_tensor("ptrB", [16, 128], F32))

                def ld_rt(gpsimd):
                    src = din["wrt"][:, :].rearrange("(kc p) e -> p kc e", p=128)
                    dRT.inc(gpsimd.dma_start(out=wrtb[:, :, :], in_=src), 16)
                B.gpsimd(ld_rt)

                def bc5(sync):
                    dB.inc(sync.dma_start(out=mB2[:, :],
                                          in_=r2_scr[0:1, :].to_broadcast([128, TOK])), 16)
                    dB.inc(sync.dma_start(out=rB2[:, :],
                                          in_=r2_scr[1:2, :].to_broadcast([128, TOK])), 16)
                B.sync(bc5)
                BC5 = dB.n

                for kc in range(KC):
                    csl = kc % 2

                    def ld_x2(sync, kc=kc, csl=csl):
                        if kc >= 2:
                            sync.wait_ge(sV.h, p5m[("sub", kc - 2)])
                        dX[csl].inc(sync.dma_start(out=x2l[csl][:, :],
                                                   in_=x2_scr[kc * 128:(kc + 1) * 128, :]), 16)
                        p5m[("x2ld", kc)] = dX[csl].n
                    B.sync(ld_x2)

                    def do_sub2(vector, kc=kc, csl=csl):
                        vector.wait_ge(dX[csl].h, p5m[("x2ld", kc)])
                        if kc == 0:
                            vector.wait_ge(dB.h, BC5)
                        if kc >= 1:
                            vector.wait_ge(sV.h, p5m[("ln2", kc - 1)])
                        p5m[("sub", kc)] = sV.inc(vector.tensor_sub(
                            t2b[:, :], x2l[csl][:, :], mB2[:, :]))
                    B.vector(do_sub2)

                    def do_mul2(vector, kc=kc):
                        vector.wait_ge(sV.h, p5m[("sub", kc)])
                        p5m[("ln2", kc)] = sV.inc(vector.tensor_mul(
                            g_xln2[:, kc, :], t2b[:, :], rB2[:, :]))
                    B.vector(do_mul2)

                if debug:
                    def dump_xln2(gpsimd):
                        gpsimd.wait_ge(sV.h, p5m[("ln2", KC - 1)])
                        for kc in range(KC):
                            dG.inc(gpsimd.dma_start(out=dbg["d_xln2"][kc * 128:(kc + 1) * 128, :],
                                                    in_=g_xln2[:, kc, :]), 16)
                        gpsimd.wait_ge(dG.h, dG.n)
                    B.gpsimd(dump_xln2)

                def ld_eb(sync):
                    dB.inc(sync.dma_start(out=ebB[:, :],
                                          in_=din["ebias"][0:1, :].to_broadcast([128, NE])), 16)
                B.sync(ld_eb)
                EB = dB.n

                def rt_mm(tensor):
                    tensor.wait_ge(dRT.h, dRT.n)
                    for kc in range(KC):
                        tensor.wait_ge(sV.h, p5m[("ln2", kc)])
                        for ts in range(2):
                            mm = tensor.matmul(pr[0:16, ts * 512:(ts + 1) * 512],
                                               wrtb[:, kc, :],
                                               g_xln2[:, kc, ts * 512:(ts + 1) * 512],
                                               start=(kc == 0), stop=(kc == KC - 1))
                    p5m["rtmm"] = sT.inc(mm)
                B.tensor(rt_mm)

                def rt_sig(scalar):
                    scalar.wait_ge(sT.h, p5m["rtmm"])
                    p5m["sig"] = sS.inc(scalar.activation(scores[:, :], pr[0:16, :], AF.Sigmoid))
                B.scalar(rt_sig)

                for tt in range(8):
                    def tr_sc(tensor, tt=tt):
                        if tt == 0:
                            tensor.wait_ge(sS.h, p5m["sig"])
                        else:
                            tensor.wait_ge(sV.h, p5m[("scev", tt - 1)])
                        p5m[("sctr", tt)] = sT.inc(tensor.transpose(
                            ptrA[:, :], scores[0:16, tt * 128:(tt + 1) * 128],
                            g_idf32[0:16, 0:16]))
                    B.tensor(tr_sc)

                    def ev_sc(vector, tt=tt):
                        vector.wait_ge(sT.h, p5m[("sctr", tt)])
                        p5m[("scev", tt)] = sV.inc(vector.tensor_copy(scT[:, tt, :], ptrA[:, :]))
                    B.vector(ev_sc)

                def topk(vector):
                    vector.wait_ge(sV.h, p5m[("scev", 7)])
                    vector.wait_ge(dB.h, EB)
                    m = sV.inc(vector.tensor_add(work[:, :, :], scT[:, :, :], _bc(ebB[:, :], 1, 8)))
                    vector.wait_ge(sV.h, m)
                    m = sV.inc(vector.memset(selacc[:, :, :], 0.0))
                    for k in range(4):
                        vector.wait_ge(sV.h, m)
                        m = sV.inc(vector.tensor_reduce(redm[:, :, 0:1], work[:, :, :],
                                                        axis=AX, op=ALU.max))
                        vector.wait_ge(sV.h, m)
                        m = sV.inc(vector.tensor_tensor(eqb[:, :, :], work[:, :, :],
                                                        redm[:, :, 0:1].to_broadcast([128, 8, NE]),
                                                        op=ALU.is_equal))
                        vector.wait_ge(sV.h, m)
                        m = sV.inc(vector.tensor_add(selacc[:, :, :], selacc[:, :, :], eqb[:, :, :]))
                        vector.wait_ge(sV.h, m)
                        m = sV.inc(vector.scalar_tensor_tensor(work[:, :, :], eqb[:, :, :],
                                                               -1e30, work[:, :, :],
                                                               op0=ALU.mult, op1=ALU.add))
                    vector.wait_ge(sV.h, m)
                    m = sV.inc(vector.tensor_mul(work[:, :, :], scT[:, :, :], selacc[:, :, :]))
                    vector.wait_ge(sV.h, m)
                    m = sV.inc(vector.tensor_reduce(redm[:, :, 0:1], work[:, :, :],
                                                    axis=AX, op=ALU.add))
                    vector.wait_ge(sV.h, m)
                    m = sV.inc(vector.tensor_scalar(redm[:, :, 1:2], redm[:, :, 0:1],
                                                    1e-20, None, op0=ALU.add))
                    vector.wait_ge(sV.h, m)
                    m = sV.inc(vector.reciprocal(redm[:, :, 0:1], redm[:, :, 1:2]))
                    vector.wait_ge(sV.h, m)
                    p5m["cmbT"] = sV.inc(vector.tensor_mul(
                        work[:, :, :], work[:, :, :],
                        redm[:, :, 0:1].to_broadcast([128, 8, NE])))
                B.vector(topk)

                for tt in range(8):
                    def tr_cm(tensor, tt=tt):
                        if tt == 0:
                            tensor.wait_ge(sV.h, p5m["cmbT"])
                        else:
                            tensor.wait_ge(sV.h, p5m[("cmev", tt - 1)])
                        p5m[("cmtr", tt)] = sT.inc(tensor.transpose(
                            ptrB[:, :], work[:, tt, :], g_idf32[:, :]))
                    B.tensor(tr_cm)

                    def ev_cm(vector, tt=tt):
                        vector.wait_ge(sT.h, p5m[("cmtr", tt)])
                        p5m[("cmev", tt)] = sV.inc(vector.tensor_copy(
                            cmb16[0:16, tt * 128:(tt + 1) * 128], ptrB[0:16, :]))
                    B.vector(ev_cm)

                def st_cmb(sync):
                    sync.wait_ge(sV.h, p5m[("cmev", 7)])
                    dR.inc(sync.dma_start(out=cmb_scr[:, :], in_=cmb16[:, :]), 16)
                    sync.wait_ge(dR.h, dR.n)
                B.sync(st_cmb)

                if debug:
                    def dump_rt(gpsimd):
                        gpsimd.wait_ge(sV.h, p5m[("cmev", 7)])
                        dG.inc(gpsimd.dma_start(out=dbg["d_scores"][:, :], in_=scores[:, :]), 16)
                        dG.inc(gpsimd.dma_start(out=dbg["d_cmb"][:, :], in_=cmb16[:, :]), 16)
                        gpsimd.wait_ge(dG.h, dG.n)
                    B.gpsimd(dump_rt)

                def r_fin(sync):
                    sync.wait_ge(sV.h, p5m[("cmev", 7)])
                    sync.wait_ge(sS.h, sS.n)
                B.sync(r_fin)

            pg = e5(nc.psum_tensor("pg", [128, 1024], F32))
            pu = e5(nc.psum_tensor("pu", [128, 1024], F32))
            pdt = [e5(nc.psum_tensor(f"pdt{i}", [128, 1024], F32)) for i in range(2)]

            NEXP = NE + 1
            for grp in range(9):
                exps = [2 * grp, 2 * grp + 1] if grp < 8 else [NE]
                for ei, e in enumerate(exps):
                    shared = e == NE
                    def ld_cb(sync, e=e, ei=ei, grp=grp, shared=shared):
                        if not shared:
                            if grp > 0:
                                sync.wait_ge(sV.h, p5m[("h", 2 * (grp - 1) + ei)])
                            dE[ei].inc(sync.dma_start(
                                out=cB5[ei][:, :],
                                in_=cmb_scr[e:e + 1, :].to_broadcast([128, TOK])), 16)
                        p5m[("cb", e)] = (ei, dE[ei].n)
                    B.sync(ld_cb)

                    for blk in range(3):
                        wsl = (3 * e + blk) % 2

                        def ld_gu(gpsimd, e=e, ei=ei, blk=blk, wsl=wsl, shared=shared, grp=grp):
                            gi = 3 * e + blk
                            if gi >= 2:
                                gpsimd.wait_ge(sT.h, p5m[("gumm", gi - 2)])
                            if shared:
                                src = din["sgut"][:, blk * 512:(blk + 1) * 512]
                            else:
                                src = din["gut"][e, :, blk * 512:(blk + 1) * 512]
                            srcr = src.rearrange("(kc p) o -> p kc o", p=128)
                            if ei == 0 and blk <= 1 and grp > 0:
                                dGA[wsl].inc(gpsimd.dma_start(
                                    out=wgu[wsl][:, 12:16, :], in_=srcr[:, 12:16, :]), 16)
                                gpsimd.wait_ge(sT.h, p5m[("dnmm", grp - 1, 2, 3)])
                                dGB[wsl].inc(gpsimd.dma_start(
                                    out=wgu[wsl][:, 0:6, :], in_=srcr[:, 0:6, :]), 16)
                                gpsimd.wait_ge(sT.h, p5m[("dnmm", grp - 1, 3, 3)])
                                dW[wsl].inc(gpsimd.dma_start(
                                    out=wgu[wsl][:, 6:12, :], in_=srcr[:, 6:12, :]), 16)
                                p5m[("gusplit", e, blk)] = (dGA[wsl].n, dGB[wsl].n)
                            else:
                                dW[wsl].inc(gpsimd.dma_start(
                                    out=wgu[wsl][:, :, :], in_=srcr), 16)
                        B.gpsimd(ld_gu)

                        for jj in range(2):
                            jfull = blk * 2 + jj

                            def gu_mm(tensor, e=e, blk=blk, jj=jj, wsl=wsl, jfull=jfull):
                                gi = 3 * e + blk
                                if jj == 0:
                                    tensor.wait_ge(dW[wsl].h, dW[wsl].n)
                                    if ("gusplit", e, blk) in p5m:
                                        na, nb = p5m[("gusplit", e, blk)]
                                        tensor.wait_ge(dGA[wsl].h, na)
                                        tensor.wait_ge(dGB[wsl].h, nb)
                                if e == 0 and blk == 0 and jj == 0:
                                    tensor.wait_ge(sV.h, p5m[("cmev", 7)])
                                    tensor.wait_ge(sV.h, p5m[("ln2", KC - 1)])
                                for ts in range(2):
                                    kg = ("gsh", e, jfull - 1, ts) if jfull > 0 else \
                                         (("gsh", e - 1, 5, ts) if e > 0 else None)
                                    if kg and kg in p5m:
                                        tensor.wait_ge(sS.h, p5m[kg])
                                    ku = ("guevh", e, jfull - 1, ts) if jfull > 0 else \
                                         (("guevh", e - 1, 5, ts) if e > 0 else None)
                                    if ku and ku in p5m:
                                        tensor.wait_ge(sV.h, p5m[ku])
                                    for kc in range(KC):
                                        tensor.matmul(pg[:, ts * 512:(ts + 1) * 512],
                                                      wgu[wsl][:, kc, jj * 256:jj * 256 + 128],
                                                      g_xln2[:, kc, ts * 512:(ts + 1) * 512],
                                                      start=(kc == 0), stop=(kc == KC - 1))
                                    for kc in range(KC):
                                        mm = tensor.matmul(pu[:, ts * 512:(ts + 1) * 512],
                                                           wgu[wsl][:, kc, jj * 256 + 128:(jj + 1) * 256],
                                                           g_xln2[:, kc, ts * 512:(ts + 1) * 512],
                                                           start=(kc == 0), stop=(kc == KC - 1))
                                    p5m[("gummh", e, jfull, ts)] = sT.inc(mm)
                                p5m[("gumm", 3 * e + blk)] = p5m[("gummj", e, jfull)] = \
                                    p5m[("gummh", e, jfull, 1)]
                            B.tensor(gu_mm)

                            for ts in range(2):
                                def gu_silu(scalar, e=e, jfull=jfull, ts=ts):
                                    scalar.wait_ge(sT.h, p5m[("gummh", e, jfull, ts)])
                                    key = ("guevh", e, jfull - 1, ts) if jfull > 0 else \
                                          (("guevh", e - 1, 5, ts) if e > 0 else None)
                                    if key and key in p5m:
                                        scalar.wait_ge(sV.h, p5m[key])
                                    p5m[("gsh", e, jfull, ts)] = sS.inc(scalar.activation(
                                        sgb[:, ts * 512:(ts + 1) * 512],
                                        pg[:, ts * 512:(ts + 1) * 512], AF.Silu))
                                B.scalar(gu_silu)

                                def gu_mul(vector, e=e, jfull=jfull, ts=ts):
                                    vector.wait_ge(sS.h, p5m[("gsh", e, jfull, ts)])
                                    key = ("hjh", e, jfull - 1, ts) if jfull > 0 else \
                                          (("hjh", e - 1, 5, ts) if e > 0 else None)
                                    if key and key in p5m:
                                        vector.wait_ge(sV.h, p5m[key])
                                    p5m[("guevh", e, jfull, ts)] = sV.inc(vector.tensor_mul(
                                        tgu[:, ts * 512:(ts + 1) * 512],
                                        sgb[:, ts * 512:(ts + 1) * 512],
                                        pu[:, ts * 512:(ts + 1) * 512]))
                                B.vector(gu_mul)

                                def h_mul(vector, e=e, ei=ei, jfull=jfull, ts=ts, shared=shared):
                                    vector.wait_ge(sV.h, p5m[("guevh", e, jfull, ts)])
                                    if not shared:
                                        cbi, cbv = p5m[("cb", e)]
                                        vector.wait_ge(dE[cbi].h, cbv)
                                        p5m[("hjh", e, jfull, ts)] = sV.inc(vector.tensor_mul(
                                            hbuf[ei][:, jfull, ts * 512:(ts + 1) * 512],
                                            tgu[:, ts * 512:(ts + 1) * 512],
                                            cB5[ei][:, ts * 512:(ts + 1) * 512]))
                                    else:
                                        p5m[("hjh", e, jfull, ts)] = sV.inc(vector.tensor_copy(
                                            hbuf[ei][:, jfull, ts * 512:(ts + 1) * 512],
                                            tgu[:, ts * 512:(ts + 1) * 512]))
                                B.vector(h_mul)
                            p5m[("hj", e, jfull)] = p5m[("hjh", e, jfull, 1)]
                    p5m[("h", e)] = p5m[("hj", e, 5)]

                nexp_g = len(exps)
                for ob in range(4):
                    for ei, e in enumerate(exps):
                        wsl = ei

                        def ld_dn(gpsimd, e=e, ei=ei, ob=ob, grp=grp, wsl=wsl):
                            if ob <= 1:
                                for e2 in exps:
                                    gpsimd.wait_ge(sT.h, p5m[("gumm", 3 * e2 + 2)])
                            else:
                                gpsimd.wait_ge(sT.h, p5m[("dnmm", grp, ob - 2, 3)])
                            if e == NE:
                                src = din["sdt"][:, ob * 512:(ob + 1) * 512]
                            else:
                                src = din["dwt"][e, :, ob * 512:(ob + 1) * 512]
                            half = 6 * (ob % 2)
                            sem = dW[wsl] if ob % 2 == 0 else dW2[wsl]
                            sem.inc(gpsimd.dma_start(
                                out=wgu[wsl][:, half:half + 6, :],
                                in_=src.rearrange("(kc p) o -> p kc o", p=128)), 16)
                            p5m[("dnld", grp, ob, wsl)] = sem.n
                        B.gpsimd(ld_dn)

                    for ot4 in range(4):
                        ot = ob * 4 + ot4

                        def dn_mm(tensor, grp=grp, ob=ob, ot4=ot4, exps=exps):
                            if ot4 == 0:
                                for wsl in range(len(exps)):
                                    sem = dW[wsl] if ob % 2 == 0 else dW2[wsl]
                                    tensor.wait_ge(sem.h, p5m[("dnld", grp, ob, wsl)])
                                if ob == 0:
                                    for e2 in exps:
                                        tensor.wait_ge(sV.h, p5m[("h", e2)])
                            key = ("dnev", grp, ob, ot4 - 2) if ot4 >= 2 else \
                                  (("dnev", grp, ob - 1, 2 + ot4) if ob > 0 else
                                   (("dnev", grp - 1, 3, 2 + ot4) if grp > 0 else None))
                            if key and key in p5m:
                                tensor.wait_ge(sV.h, p5m[key])
                            pdx = pdt[ot4 % 2]
                            half = 6 * (ob % 2)
                            for ei in range(len(exps)):
                                for kc in range(6):
                                    for ts in range(2):
                                        mm = tensor.matmul(
                                            pdx[:, ts * 512:(ts + 1) * 512],
                                            wgu[ei][:, half + kc, ot4 * 128:(ot4 + 1) * 128],
                                            hbuf[ei][:, kc, ts * 512:(ts + 1) * 512],
                                            start=(ei == 0 and kc == 0),
                                            stop=(ei == len(exps) - 1 and kc == 5))
                            p5m[("dnmm", grp, ob, ot4)] = sT.inc(mm)
                        B.tensor(dn_mm)

                        if grp == 8:
                            osl = ot % 2

                            def ld_xr5(sync, ot=ot, osl=osl):
                                if ot >= 2:
                                    sync.wait_ge(dO.h, dO.n)
                                else:
                                    sync.wait_ge(sV.h, p5m[("sub", KC - 1)])
                                dX[osl].inc(sync.dma_start(
                                    out=xr5[osl][:, :],
                                    in_=x2_scr[ot * 128:(ot + 1) * 128, :]), 16)
                            B.sync(ld_xr5)

                            def fin_ev(vector, grp=grp, ob=ob, ot4=ot4, ot=ot, osl=osl):
                                vector.wait_ge(sT.h, p5m[("dnmm", grp, ob, ot4)])
                                vector.wait_ge(dX[osl].h, dX[osl].n)
                                if ot >= 2:
                                    vector.wait_ge(dO.h, dO.n)
                                pdx = pdt[ot4 % 2]
                                m1 = sV.inc(vector.scalar_tensor_tensor(
                                    obuf[osl][:, :], pdx[:, :], 1.0,
                                    acc[:, ot, :], op0=ALU.mult, op1=ALU.add))
                                vector.wait_ge(sV.h, m1)
                                m1 = sV.inc(vector.tensor_add(
                                    obuf[osl][:, :], obuf[osl][:, :], xr5[osl][:, :]))
                                p5m[("dnev", grp, ob, ot4)] = m1
                            B.vector(fin_ev)

                            def st_out(sync, ot=ot, osl=osl, grp=grp, ob=ob, ot4=ot4):
                                sync.wait_ge(sV.h, p5m[("dnev", grp, ob, ot4)])
                                dO.inc(sync.dma_start(out=out_hbm[ot * 128:(ot + 1) * 128, :],
                                                      in_=obuf[osl][:, :]), 16)
                            B.sync(st_out)
                        else:
                            def dn_ev(vector, grp=grp, ob=ob, ot4=ot4, ot=ot):
                                vector.wait_ge(sT.h, p5m[("dnmm", grp, ob, ot4)])
                                pdx = pdt[ot4 % 2]
                                if grp == 0:
                                    p5m[("dnev", grp, ob, ot4)] = sV.inc(
                                        vector.tensor_copy(acc[:, ot, :], pdx[:, :]))
                                else:
                                    p5m[("dnev", grp, ob, ot4)] = sV.inc(
                                        vector.scalar_tensor_tensor(
                                            acc[:, ot, :], pdx[:, :], 1.0,
                                            acc[:, ot, :], op0=ALU.mult, op1=ALU.add))
                            B.vector(dn_ev)

            def p5_fin(sync):
                sync.wait_ge(dO.h, dO.n)
            B.sync(p5_fin)

    c45.close()
    ctx.close()

    return nc, dbg



def _prep_shared(L0_attn_in_proj_w, L0_attn_conv_w, L0_attn_conv_b, L0_attn_dt_bias,
                 L0_attn_A_log, L0_attn_D, L0_attn_norm_w, L0_attn_out_proj_w,
                 L0_ffn_router_w, L0_ffn_e_bias, L0_ffn_gate_up_proj, L0_ffn_down_proj,
                 L0_ffn_shared_gate_w, L0_ffn_shared_up_w, L0_ffn_shared_down_w):
    C = np.ascontiguousarray
    A = np.asarray
    sh = {}
    sh["w1t"] = C(A(L0_attn_in_proj_w).T)
    sh["convw"] = C(A(L0_attn_conv_w).reshape(NCC, 128, 4).transpose(1, 0, 2).reshape(128, NCC * 4))
    sh["convb"] = C(A(L0_attn_conv_b).reshape(NCC, 128).T)
    cw = A(L0_attn_conv_w).reshape(NCC, 128, 4)
    cb5 = A(L0_attn_conv_b).reshape(NCC, 128)
    d4 = np.zeros((5, NCC, 128, 128), np.float32)
    ii = np.arange(128)
    for tap in range(4):
        for cc in range(NCC):
            d4[tap, cc, ii, ii] = cw[cc, :, tap]
    for cc in range(NCC):
        d4[4, cc, ii, ii] = cb5[cc]
    sh["convd"] = C(d4.reshape(5 * NCC * 128, 128))
    sh["dtb"] = C(A(L0_attn_dt_bias).reshape(NH, 1))
    sh["alog"] = C(A(L0_attn_A_log).reshape(NH, 1))
    sh["dpp"] = C(np.repeat(A(L0_attn_D), HD).reshape(32, 128).T)
    sh["normw"] = C(A(L0_attn_norm_w).reshape(32, 128).T)
    sh["w2t"] = C(A(L0_attn_out_proj_w).T)
    sh["wrt"] = C(A(L0_ffn_router_w).T)
    sh["ebias"] = C(A(L0_ffn_e_bias).reshape(1, NE))
    gu = A(L0_ffn_gate_up_proj)
    g = gu[:, :IE, :]
    u = gu[:, IE:, :]
    inter = np.empty((NE, 2 * IE, D), np.float32)
    for j in range(IE // 128):
        inter[:, j * 256:j * 256 + 128, :] = g[:, j * 128:(j + 1) * 128, :]
        inter[:, j * 256 + 128:(j + 1) * 256, :] = u[:, j * 128:(j + 1) * 128, :]
    sh["gut"] = C(inter.transpose(0, 2, 1))
    sh["dwt"] = C(A(L0_ffn_down_proj).transpose(0, 2, 1))
    sg = A(L0_ffn_shared_gate_w)
    su = A(L0_ffn_shared_up_w)
    sinter = np.empty((2 * IE, D), np.float32)
    for j in range(IE // 128):
        sinter[j * 256:j * 256 + 128, :] = sg[j * 128:(j + 1) * 128, :]
        sinter[j * 256 + 128:(j + 1) * 256, :] = su[j * 128:(j + 1) * 128, :]
    sh["sgut"] = C(sinter.T)
    sh["sdt"] = C(A(L0_ffn_shared_down_w).T)
    return sh


def _core_inputs(x_full, sh):
    maps = []
    for core in range(8):
        b, h = core // 2, core % 2
        xm = np.ascontiguousarray(x_full[b, h * TOK:(h + 1) * TOK, :].T)
        if h == 0:
            xh = np.zeros((D, TOK), np.float32)
            sm = np.zeros((128, 1), np.float32)
        else:
            xh = np.ascontiguousarray(x_full[b, 0:TOK, :].T)
            sm = np.ones((128, 1), np.float32)
        m = dict(sh)
        m["xin"] = np.ascontiguousarray(np.concatenate([xh, xm], axis=1))
        m["smask"] = sm
        maps.append(m)
    return maps


def kernel(L0_attn_hidden_states, **kw):
    x = np.asarray(L0_attn_hidden_states)
    sh = _prep_shared(**kw)
    nc, _ = build(debug=False)
    maps = _core_inputs(x, sh)
    res = run_bass_kernel_spmd(nc, maps, core_ids=list(range(8)))
    out = np.empty((4, 2048, 2048), np.float32)
    for core in range(8):
        b, h = core // 2, core % 2
        out[b, h * TOK:(h + 1) * TOK, :] = res.results[core]["out"].T
    return out
